# version 10
# baseline (speedup 1.0000x reference)
import math
import contextlib
import numpy as np
import ml_dtypes
import concourse.bass as bass
import concourse.mybir as mybir
from concourse.bass_utils import run_bass_kernel_spmd

F32 = mybir.dt.float32
BF16 = mybir.dt.bfloat16
AF = mybir.ActivationFunctionType
AX = mybir.AxisListType
ALU = mybir.AluOpType
BF = ml_dtypes.bfloat16

D = 2048
S = 2048
NSEQ = 2
NCH = 16
NST = 4
TS = 512
NHC = 16
NEXP = 7
BIG = 1.0e9
ISCALE = 1.0 / math.sqrt(32.0)
EPS = 1.0e-5


class Ch:
    __slots__ = ("sem", "n", "name")

    def __init__(self, sem, name):
        self.sem = sem
        self.n = 0
        self.name = name


class Em:
    ENGS = ("pe", "act", "dve", "sp", "pool")

    def __init__(self):
        self.q = {e: [] for e in self.ENGS}
        self.wcache = {e: {} for e in self.ENGS}
        self.gates = {}
        self.autochain = {}

    def wait(self, eng, ch, val=None):
        val = ch.n if val is None else val
        assert val <= ch.n, f"wait on future inc of {ch.name}: {val} > {ch.n}"
        if val <= 0:
            return
        c = self.wcache[eng]
        if c.get(id(ch), -1) >= val:
            return
        c[id(ch)] = val
        self.q[eng].append(("w", ch.sem, val))

    def op(self, eng, meth, *args, incs=(), **kw):
        ac = self.autochain.get(eng)
        if ac is not None:
            self.wait(eng, ac, ac.n)
            if not incs:
                incs = ((ac, 1),)
            else:
                assert len(incs) == 1 and incs[0][0] is ac, \
                    f"{eng} op must inc only the self-channel"
        lst = []
        for ch, k in incs:
            ch.n += k
            lst.append((ch.sem, k))
        self.q[eng].append(("o", meth, args, kw, lst))

    def acquire(self, eng, key):
        for ch, val in self.gates.get(key, ()):
            self.wait(eng, ch, val)

    def release(self, key, *pairs):
        out = []
        for p in pairs:
            if isinstance(p, tuple):
                out.append((p[0], p[1] if p[1] is not None else p[0].n))
            else:
                out.append((p, p.n))
        self.gates[key] = out

    def replay(self, eng, lst):
        for it in lst:
            if it[0] == "w":
                eng.wait_ge(it[1], it[2])
            else:
                _, meth, args, kw, incs = it
                ins = getattr(eng, meth)(*args, **kw)
                for sem, k in incs:
                    ins = ins.then_inc(sem, k)


def build_nc(upto="all", dbg=False, nmats=3, nmch=16, nseq=NSEQ):
    PH = {"p0": 0, "p1": 1, "p2": 2, "p3": 3, "p4": 4, "p5": 5, "all": 6}[upto]
    nc = bass.Bass("TRN2", target_bir_lowering=False,
                   dynamic_dma_scratch_size=16384)

    dt_i = "ExternalOutput" if dbg else "Internal"
    xT = nc.dram_tensor("xT", [NSEQ, D, S], F32, kind="ExternalInput").ap()
    wq = nc.dram_tensor("wq", [16, 128, D], F32, kind="ExternalInput").ap()
    wk = nc.dram_tensor("wk", [16, 128, D], F32, kind="ExternalInput").ap()
    wv = nc.dram_tensor("wv", [16, 128, D], F32, kind="ExternalInput").ap()
    wo = nc.dram_tensor("wo", [16, 128, D], F32, kind="ExternalInput").ap()
    w1 = nc.dram_tensor("w1", [NEXP, 16, 128, D], F32, kind="ExternalInput").ap()
    wg = nc.dram_tensor("wg", [NEXP, 16, 128, D], F32, kind="ExternalInput").ap()
    w2 = nc.dram_tensor("w2", [NEXP, 16, 128, D], F32, kind="ExternalInput").ap()
    ws1 = nc.dram_tensor("ws1", [16, 128, D], F32, kind="ExternalInput").ap()
    wsg = nc.dram_tensor("wsg", [16, 128, D], F32, kind="ExternalInput").ap()
    ws2 = nc.dram_tensor("ws2", [16, 128, D], F32, kind="ExternalInput").ap()
    gw = nc.dram_tensor("gw", [128, 112], F32, kind="ExternalInput").ap()
    bp = nc.dram_tensor("bp", [128, 80], F32, kind="ExternalInput").ap()
    out = nc.dram_tensor("out", [NSEQ, 16, 128, S], F32, kind="ExternalOutput").ap()
    qh = nc.dram_tensor("qh", [NSEQ, 16, 128, S], BF16, kind=dt_i).ap()
    kh = nc.dram_tensor("kh", [NSEQ, 16, 128, S], BF16, kind=dt_i).ap()
    vh = nc.dram_tensor("vh", [NSEQ, 16, 128, S], BF16, kind=dt_i).ap()
    acch = nc.dram_tensor("acch", [NSEQ, 16, 128, S], F32,
                          kind=("ExternalOutput" if (dbg and PH >= 5) else "Internal")).ap()
    dbgf = dbgy = dbgw = None
    if dbg and PH >= 2:
        dbgf = nc.dram_tensor("dbgf", [NSEQ, 16, 128, S], BF16, kind="ExternalOutput").ap()
    if dbg and PH >= 3:
        dbgy = nc.dram_tensor("dbgy", [NSEQ, 16, 128, S], BF16, kind="ExternalOutput").ap()
    if dbg and PH >= 4:
        dbgw = nc.dram_tensor("dbgw", [NSEQ, 7, S], BF16, kind="ExternalOutput").ap()

    ones_np = np.ones((128, 128), dtype=BF)
    mab_np = np.zeros((128, 128), dtype=BF)
    mab_np[0:64, 0:64] = 1
    mab_np[64:128, 64:128] = 1
    id_np = np.eye(128, dtype=np.float32)
    ind_np = np.zeros((7, 7 * 128), dtype=BF)
    for e in range(7):
        ind_np[e, e * 128:(e + 1) * 128] = 1
    c_ones = nc.inline_tensor(ones_np, "c_ones").ap()
    c_mab = nc.inline_tensor(mab_np, "c_mab").ap()
    c_id = nc.inline_tensor(id_np, "c_id").ap()
    c_ind = nc.inline_tensor(ind_np, "c_ind").ap()

    em = Em()
    sems = []

    def mkch(name):
        h = nc.semaphore(name)
        sem = h.__enter__()
        sems.append(h)
        return Ch(sem, name)

    ctx = contextlib.ExitStack()

    def sb(name, shape, dt):
        return ctx.enter_context(nc.sbuf_tensor(name, shape, dt))

    def pst(name, shape):
        return ctx.enter_context(nc.psum_tensor(name, shape, F32))

    S1 = sb("S1", [128, NCH * S], BF16)
    S2 = sb("S2", [128, NCH * S], BF16)
    expS = sb("expS", [128, 3072], BF16)
    qhc = [sb(f"qhc{i}", [128, S], BF16) for i in range(2)]
    khc = [sb(f"khc{i}", [128, S], BF16) for i in range(2)]
    vsb = [sb(f"vsb{i}", [128, S], BF16) for i in range(2)]
    wpA = [sb(f"wpA{i}", [128, D], BF16) for i in range(2)]
    wpB = [sb(f"wpB{i}", [128, D], BF16) for i in range(2)]
    gwt = sb("gwt", [128, 112], BF16)
    bpt = sb("bpt", [128, 80], F32)
    onest = sb("onest", [128, 128], BF16)
    mabt = sb("mabt", [128, 128], BF16)
    idt = sb("idt", [128, 128], F32)
    ind7t = sb("ind7t", [7, 896], BF16)
    esb = [sb(f"esb{i}", [128, TS], BF16) for i in range(2)]
    stmp = [sb(f"stmp{i}", [128, TS], F32) for i in range(2)]
    recipb = [sb(f"recip{i}", [128, TS], F32) for i in range(2)]
    meanb = sb("meanb", [128, TS], F32)
    rstdb = sb("rstdb", [128, TS], F32)
    stage = [sb(f"stage{i}", [128, TS], F32) for i in range(2)]
    astg = recipb
    ostg = [stage[0][:], stage[1][:]]
    wTt = sb("wTt", [7, S], BF16)
    wbt = sb("wbt", [128, S], BF16)
    gbb = sb("gbb", [128, 7], F32)
    bbb = sb("bbb", [128, 7], F32)
    grow = sb("grow", [1, 7], BF16)
    brow = sb("brow", [1, 7], BF16)
    g1 = sb("g1", [128, 7], F32)
    g2 = sb("g2", [128, 7], F32)
    g3 = sb("g3", [128, 7], F32)
    g4 = sb("g4", [128, 7], F32)
    gs1 = sb("gs1", [128, 1], F32)
    gs2 = sb("gs2", [128, 1], F32)
    wtok = sb("wtok", [128, 7], F32)
    tr1 = sb("tr1", [128, 1], F32)
    qn2a = sb("qn2a", [128, 16], F32)
    kn2a = sb("kn2a", [128, 16], F32)
    pm16 = sb("pm16", [128, 16], F32)
    c16 = sb("c16", [16, 1], F32)
    crow = sb("crow", [1, 16], BF16)
    cba = sb("cba", [128, 16], F32)

    PA = [pst("PA0", [128, 1024]), pst("PA1", [128, 1024])]
    PB = [pst(f"PB{i}", [128, TS]) for i in range(4)]

    cXL = mkch("cXL")
    cW = {"A0": mkch("cWA0"), "A1": mkch("cWA1"), "B0": mkch("cWB0"), "B1": mkch("cWB1")}
    cSP = [mkch("cSP0"), mkch("cSP1")]
    cHL = [mkch("cHL0"), mkch("cHL1")]
    cAS = [mkch("cAS0"), mkch("cAS1")]
    cAL = [mkch("cAL0"), mkch("cAL1")]
    cOS = [mkch("cOS0"), mkch("cOS1")]
    cMS = mkch("cMS")
    cDB = mkch("cDB")
    eAC = mkch("eAC")
    eDV = mkch("eDV")
    em.autochain = {"act": eAC, "dve": eDV}
    eEV = eEX = eSQ = eSI = eAC
    eCd = eRD = eNM = eFE = eGW = eWS = eWBS = eAP = eSG = eR2 = eY2 = eDV
    ePQ = mkch("ePQ")
    eSC = mkch("eSC")
    eAV = mkch("eAV")
    eHC = mkch("eHC")
    eC = mkch("eC")
    eAT = mkch("eAT")
    eSM = mkch("eSM")
    eGL = mkch("eGL")
    eWT = mkch("eWT")
    eWB = mkch("eWB")
    eH1 = mkch("eH1")
    eHG = mkch("eHG")
    eW2 = mkch("eW2")
    eP0 = mkch("eP0")
    ePS = mkch("ePS")

    W = em.wait
    O = em.op

    for dst, src in ((onest, c_ones), (mabt, c_mab), (idt, c_id), (ind7t, c_ind),
                     (gwt, gw), (bpt, bp), (grow, bp[0:1, 64:71]), (brow, bp[0:1, 71:78])):
        O("pool", "dma_start", out=dst[:], in_=src, incs=[(cMS, 16)])
    W("pe", cMS)
    O("pe", "matmul", PB[0][:, 0:7], onest[0:1, 0:128], grow[:], start=True, stop=True,
      skip_group_check=True, incs=[(eP0, 1)])
    O("pe", "matmul", PB[1][:, 0:7], onest[0:1, 0:128], brow[:], start=True, stop=True,
      skip_group_check=True, incs=[(eP0, 1)])
    W("dve", eP0)
    O("dve", "tensor_copy", gbb[:], PB[0][:, 0:7], incs=[(eCd, 1)])
    O("dve", "tensor_copy", bbb[:], PB[1][:, 0:7], incs=[(eCd, 1)])
    em.release(("PB", 0), eCd)
    em.release(("PB", 1), eCd)

    def ln_stats_finalize():
        W("act", eDV)
        W("dve", eSM)
        O("dve", "tensor_scalar_mul", meanb[:], PB[2][:, :], 1.0 / 2048.0)
        O("dve", "tensor_tensor", rstdb[:], meanb[:], meanb[:], ALU.mult)
        O("dve", "scalar_tensor_tensor", rstdb[:], PB[3][:, :], 1.0 / 2048.0, rstdb[:],
          ALU.mult, ALU.subtract)
        em.release(("PB", 2), eDV)
        em.release(("PB", 3), eDV)
        W("act", eDV)
        O("act", "activation", rstdb[:], rstdb[:], AF.Ln, bias=bpt[:, 78:79], scale=1.0)
        O("act", "activation", rstdb[:], rstdb[:], AF.Exp, bias=0.0, scale=-0.5)
        return eAC.n

    def ln_stats_pass(getr, rdy=None):
        em.acquire("pe", ("PB", 2))
        em.acquire("pe", ("PB", 3))
        for m in range(16):
            epar = m % 2
            if rdy is not None:
                W("act", rdy[m][0], rdy[m][1])
            em.acquire("act", ("esb", epar))
            O("act", "activation", esb[epar][:], getr(m), AF.Square, incs=[(eSQ, 1)])
            W("pe", eSQ)
            O("pe", "matmul", PB[2][:, :], onest[:], getr(m), start=(m == 0),
              stop=(m == 15), skip_group_check=True)
            O("pe", "matmul", PB[3][:, :], onest[:], esb[epar][:], start=(m == 0),
              stop=(m == 15), skip_group_check=True, incs=[(eSM, 1)])
            em.release(("esb", epar), eSM)
        return ln_stats_finalize()

    def emit_p6_st(ps, pstri, ps1t, prdy, st):
        W("sp", cAS[0], prdy[st][0])
        W("sp", cAS[1], prdy[st][1])
        rdy = []
        for m in range(16):
            apar = m % 2
            em.acquire("sp", ("rbuf", apar))
            O("sp", "dma_start", out=astg[apar][:],
              in_=acch[ps, m, :, st * TS:(st + 1) * TS], incs=[(cAL[apar], 16)])
            W("dve", cAL[apar])
            O("dve", "tensor_tensor", ps1t(m, st), astg[apar][:], ps1t(m, st),
              ALU.add)
            rdy.append((eDV, eDV.n))
            em.release(("rbuf", apar), eDV)
        ssv = ln_stats_pass(lambda m: ps1t(m, st), rdy)
        for m in range(16):
            opar = m % 2
            W("dve", eAC, ssv)
            em.acquire("dve", ("stage", opar))
            O("dve", "tensor_tensor", stmp[0][:], ps1t(m, st), meanb[:], ALU.subtract)
            O("dve", "tensor_tensor", ostg[opar], stmp[0][:], rstdb[:], ALU.mult)
            W("sp", eY2)
            O("sp", "dma_start", out=out[ps, m, :, st * TS:(st + 1) * TS],
              in_=ostg[opar], incs=[(cOS[opar], 16)])
            em.release(("stage", opar), cOS[opar])
        if st == NST - 1:
            for c in range(NCH):
                em.release(("buf", pstri, c), eDV)

    for s in range(nseq if PH >= 1 else 0):
        stri = s % 2
        ftsi = 1 - stri
        STR = S1 if stri == 0 else S2
        FTS = S2 if stri == 0 else S1

        def s1t(c, st, _B=STR):
            a = c * S + st * TS
            return _B[:, a:a + TS]

        def s2t(c, st, _B=FTS):
            a = c * S + st * TS
            return _B[:, a:a + TS]

        for c in range(NCH):
            em.acquire("pool", ("buf", stri, c))
            O("pool", "dma_start", out=STR[:, c * S:(c + 1) * S],
              in_=xT[s, c * 128:(c + 1) * 128, :], incs=[(cXL, 16)])
        xl_done = cXL.n

        p1b = 0
        ep1 = 0
        np1 = 0
        nsq = [0]
        npend = []

        def flush_norm(k=1):
            nonlocal np1
            for _ in range(k):
                if not npend:
                    return
                sq, dst, first, sq_dv = npend.pop(0)
                pa_ = np1 % 2
                np1 += 1
                em.acquire("pe", ("PA", pa_))
                W("pe", eDV, sq_dv)
                O("pe", "matmul", PA[pa_][:, 0:512], mabt[:], sq, start=True,
                  stop=True, skip_group_check=True, incs=[(eC, 1)])
                em.release(("wbt", slot), eC)
                W("dve", eC)
                O("dve", "reduce_max", tr1[:], PA[pa_][:, 0:512], axis=AX.X)
                em.release(("PA", pa_), eDV)
                if first:
                    O("dve", "tensor_copy", dst, tr1[:])
                else:
                    O("dve", "tensor_max", dst, tr1[:], dst)

        trip = ((wq, qh, 0), (wk, kh, 16), (wv, vh, 32))
        spill_rdy = {}
        cb_rdy = {}
        p1c = [0]

        def emit_cb(m):
            O("dve", "tensor_tensor", tr1[:], qn2a[:, m:m + 1], kn2a[:, m:m + 1],
              ALU.add)
            O("dve", "tensor_scalar_mul", tr1[:], tr1[:], -0.5 * ISCALE)
            em.acquire("pe", ("PA", 0))
            W("pe", eDV)
            O("pe", "transpose", PA[0][0:1, 0:128], tr1[:], idt[:], incs=[(eC, 1)])
            W("dve", eC)
            O("dve", "tensor_reduce", c16[0:1, 0:1], PA[0][0:1, 0:128], AX.X, ALU.min)
            em.release(("PA", 0), eDV)
            O("dve", "tensor_copy", crow[0:1, 0:1], c16[0:1, 0:1])
            em.acquire("pe", ("PA", 1))
            W("pe", eDV)
            O("pe", "matmul", PA[1][:, 0:1], onest[0:1, 0:128], crow[0:1, 0:1],
              start=True, stop=True, skip_group_check=True, incs=[(eC, 1)])
            W("dve", eC)
            O("dve", "tensor_copy", cba[:, m:m + 1], PA[1][:, 0:1])
            em.release(("PA", 1), eDV)

        def emit_p2_loads(hc):
            par = hc % 2
            em.acquire("sp", ("hcq", par))
            W("sp", cSP[0], spill_rdy[hc][0])
            W("sp", cSP[1], spill_rdy[hc][1])
            O("sp", "dma_start", out=qhc[par][:], in_=qh[s, hc], incs=[(cHL[par], 16)])
            O("sp", "dma_start", out=khc[par][:], in_=kh[s, hc], incs=[(cHL[par], 16)])
            em.acquire("sp", ("hcv", par))
            for j in range(16):
                O("sp", "dma_start_transpose", vsb[par][:, j * 128:(j + 1) * 128],
                  vh[s, hc, :, j * 128:(j + 1) * 128], incs=[(cHL[par], 16)])
            return cHL[par].n

        def p2_gen(hc, hl_done):
            par = hc % 2
            W("pe", cHL[par], hl_done)
            W("act", eDV, cb_rdy[hc])
            pend = [None]

            def emit_rhoav(it):
                st_, j_, slot_, exn = it
                pr, pf = 2, 3
                W("pe", eEX, exn)
                if j_ == 0:
                    em.acquire("pe", ("PB", pr))
                    em.acquire("pe", ("PB", pf))
                eslot = expS[:, slot_ * 1024:(slot_ + 1) * 1024]
                O("pe", "matmul", PB[pr][0:64, :], onest[:, 0:64], eslot[:, 0:512],
                  start=(j_ == 0), stop=(j_ == 15), skip_group_check=True,
                  tile_position=(0, 0))
                O("pe", "matmul", PB[pr][64:128, :], onest[:, 64:128], eslot[:, 512:1024],
                  start=(j_ == 0), stop=(j_ == 15), skip_group_check=True,
                  tile_position=(0, 64))
                O("pe", "matmul", PB[pf][0:64, :], vsb[par][:, j_ * 128:j_ * 128 + 64],
                  eslot[:, 0:512], start=(j_ == 0), stop=(j_ == 15),
                  skip_group_check=True, tile_position=(0, 0), incs=[(eAV, 1)])
                O("pe", "matmul", PB[pf][64:128, :], vsb[par][:, j_ * 128 + 64:(j_ + 1) * 128],
                  eslot[:, 512:1024], start=(j_ == 0), stop=(j_ == 15),
                  skip_group_check=True, tile_position=(0, 64), incs=[(eAV, 1)])
                em.release(("expS", slot_), eAV)
                if j_ == 15:
                    st2par = st_ % 2
                    W("dve", eAV)
                    em.acquire("dve", ("rbuf", st2par))
                    O("dve", "reciprocal", recipb[st2par][:], PB[pr][:, :])
                    O("dve", "tensor_tensor", s2t(hc, st_), PB[pf][:, :],
                      recipb[st2par][:], ALU.mult, incs=[(eFE, 1)])
                    em.release(("rbuf", st2par), eFE)
                    em.release(("PB", pr), eFE)
                    em.release(("PB", pf), eFE)

            glb = 0
            for st in range(NST):
                for j in range(16):
                    slot = glb % 3
                    pa = glb % 2
                    glb += 1
                    em.acquire("pe", ("PA", pa))
                    O("pe", "matmul", PA[pa][:, 0:512], khc[par][0:64, j * 128:(j + 1) * 128],
                      qhc[par][0:64, st * TS:(st + 1) * TS], start=True, stop=True,
                      skip_group_check=True, tile_position=(0, 0))
                    O("pe", "matmul", PA[pa][:, 512:1024],
                      khc[par][64:128, j * 128:(j + 1) * 128],
                      qhc[par][64:128, st * TS:(st + 1) * TS], start=True, stop=True,
                      skip_group_check=True, tile_position=(64, 0), incs=[(eSC, 1)])
                    W("act", eSC)
                    em.acquire("act", ("expS", slot))
                    O("act", "activation", expS[:, slot * 1024:(slot + 1) * 1024],
                      PA[pa][:, :], AF.Exp, bias=cba[:, hc:hc + 1], scale=ISCALE,
                      incs=[(eEX, 1)])
                    em.release(("PA", pa), eEX)
                    if pend[0] is not None:
                        emit_rhoav(pend[0])
                    pend[0] = (st, j, slot, eEX.n)
                if st == NST - 1:
                    emit_rhoav(pend[0])
                    em.release(("hcq", par), eSC)
                    em.release(("hcv", par), eAV)
                yield

        for m in range(nmch):
            gen = None
            if PH >= 2 and m >= 1 and nmats == 3:
                gen = p2_gen(m - 1, emit_p2_loads(m - 1))
            gi = 0
            for mat, spill, bcol in trip[:nmats]:
                par = str(p1c[0] % 2)
                wb_ = wpA[p1c[0] % 2]
                p1c[0] += 1
                em.acquire("pool", ("wpA", par))
                O("pool", "dma_start", out=wb_[:], in_=mat[m],
                  incs=[(cW["A" + par], 16)])
                W("pe", cW["A" + par])
                W("pe", cXL, xl_done)
                for st in range(NST):
                    bk = p1b % 2
                    p1b += 1
                    em.acquire("pe", ("PB", bk))
                    for kc in range(16):
                        O("pe", "matmul", PB[bk][:, :], wb_[:, kc * 128:(kc + 1) * 128],
                          s1t(kc, st), start=(kc == 0), stop=(kc == 15),
                          skip_group_check=True, incs=[(ePQ, 1)] if kc == 15 else [])
                    epar = ep1 % 2
                    ep1 += 1
                    W("dve", ePQ)
                    em.acquire("dve", ("esb", epar))
                    O("dve", "tensor_scalar_add", esb[epar][:], PB[bk][:, :],
                      bpt[:, bcol + m:bcol + m + 1])
                    evac_dv = eDV.n
                    em.release(("PB", bk), eDV)
                    W("sp", eDV, evac_dv)
                    O("sp", "dma_start", out=spill[s, m, :, st * TS:(st + 1) * TS],
                      in_=esb[epar][:], incs=[(cSP[epar], 16)])
                    if mat is not wv:
                        slot = nsq[0] % 4
                        nsq[0] += 1
                        em.acquire("dve", ("wbt", slot))
                        sq = wbt[:, slot * 512:(slot + 1) * 512]
                        O("dve", "tensor_tensor", sq, esb[epar][:], esb[epar][:],
                          ALU.mult)
                        em.release(("esb", epar), cSP[epar], eDV)
                        npend.append((sq, (qn2a if mat is wq else kn2a)[:, m:m + 1],
                                      st == 0, eDV.n))
                    else:
                        em.release(("esb", epar), cSP[epar])
                    flush_norm()
                    gi += 1
                    if gen is not None and gi % 3 == 0:
                        next(gen)
                em.release(("wpA", par), ePQ)
            if nmats == 3 and PH >= 2:
                flush_norm(len(npend))
                emit_cb(m)
                cb_rdy[m] = eDV.n
            spill_rdy[m] = (cSP[0].n, cSP[1].n)
        flush_norm(len(npend))

        sp_q0 = cSP[0].n
        sp_q1 = cSP[1].n
        if PH < 2:
            for c in range(NCH):
                em.release(("buf", stri, c), ePQ)
            continue
        gen = p2_gen(nmch - 1, emit_p2_loads(nmch - 1))
        for _ in range(NST):
            next(gen)

        fe_all = eFE.n
        if dbg and PH >= 2:
            W("sp", eFE, fe_all)
            for c in range(NCH):
                O("sp", "dma_start", out=dbgf[s, c], in_=FTS[:, c * S:(c + 1) * S],
                  incs=[(cDB, 16)])
            W("dve", cDB)
        if PH < 3:
            for c in range(NCH):
                em.release(("buf", stri, c), cDB if dbg else eFE)
                em.release(("buf", ftsi, c), cDB if dbg else eFE)
            continue

        p3b = 0
        for m in range(16):
            par = str(m % 2)
            em.acquire("pool", ("wpA", par))
            O("pool", "dma_start", out=wpA[m % 2][:], in_=wo[m],
              incs=[(cW["A" + par], 16)])
            W("pe", cW["A" + par])
            W("pe", eFE, fe_all)
            for st in range(NST):
                bk = p3b % 2
                p3b += 1
                em.acquire("pe", ("PB", bk))
                for kc in range(16):
                    O("pe", "matmul", PB[bk][:, :], wpA[m % 2][:, kc * 128:(kc + 1) * 128],
                      s2t(kc, st), start=(kc == 0), stop=(kc == 15),
                      skip_group_check=True, incs=[(eAT, 1)] if kc == 15 else [])
                W("dve", eAT)
                O("dve", "scalar_tensor_tensor", s1t(m, st), PB[bk][:, :],
                  bpt[:, 48 + m:48 + m + 1], s1t(m, st), ALU.add, ALU.add,
                  incs=[(eRD, 1)])
                em.release(("PB", bk), eRD)
            em.release(("wpA", par), eAT)
        for st in range(NST):
            W("act", eDV)
            ssv = ln_stats_pass(lambda m: s1t(m, st))
            for m in range(16):
                W("dve", eAC, ssv)
                O("dve", "tensor_tensor", stmp[0][:], s1t(m, st), meanb[:], ALU.subtract)
                O("dve", "tensor_tensor", s1t(m, st), stmp[0][:], rstdb[:], ALU.mult)
        nm_all = eNM.n
        if dbg and PH >= 3:
            W("sp", eNM, nm_all)
            for c in range(NCH):
                O("sp", "dma_start", out=dbgy[s, c], in_=STR[:, c * S:(c + 1) * S],
                  incs=[(cDB, 16)])
            W("dve", cDB)
        if PH < 4:
            for c in range(NCH):
                em.release(("buf", stri, c), cDB if dbg else eNM)
                em.release(("buf", ftsi, c), cDB if dbg else eNM)
            continue

        for sc in range(16):
            bk = sc % 2
            em.acquire("pe", ("PB", bk))
            W("pe", eNM, nm_all)
            for kc in range(16):
                O("pe", "matmul", PB[bk][:, 0:7],
                  STR[:, kc * S + sc * 128:kc * S + (sc + 1) * 128],
                  gwt[:, kc * 7:(kc + 1) * 7], start=(kc == 0), stop=(kc == 15),
                  skip_group_check=True, incs=[(eGL, 1)] if kc == 15 else [])
            W("dve", eGL)
            O("dve", "tensor_tensor", g1[:], PB[bk][:, 0:7], gbb[:], ALU.add,
              incs=[(eCd, 1)])
            em.release(("PB", bk), eCd)
            O("dve", "reduce_max", gs1[:], g1[:], axis=AX.X)
            O("dve", "tensor_single_scalar", g1[:], g1[:], gs1[:], ALU.subtract,
              incs=[(eCd, 1)])
            W("act", eCd)
            O("act", "activation", g2[:], g1[:], AF.Exp, bias=0.0, scale=1.0,
              incs=[(eSQ, 1)])
            W("dve", eSQ)
            O("dve", "reduce_sum", gs1[:], g2[:], axis=AX.X)
            O("dve", "reciprocal", gs2[:], gs1[:])
            O("dve", "tensor_single_scalar", g2[:], g2[:], gs2[:], ALU.mult)
            O("dve", "tensor_tensor", g4[:], g2[:], bbb[:], ALU.add)
            O("dve", "tensor_copy", g1[:], g4[:])
            for _ in range(2):
                O("dve", "reduce_max", gs1[:], g1[:], axis=AX.X)
                O("dve", "tensor_single_scalar", g3[:], g1[:], gs1[:], ALU.is_ge)
                O("dve", "scalar_tensor_tensor", g1[:], g3[:], -BIG, g1[:],
                  ALU.mult, ALU.add)
            O("dve", "reduce_max", gs1[:], g1[:], axis=AX.X)
            O("dve", "tensor_single_scalar", g3[:], g4[:], gs1[:], ALU.is_ge)
            O("dve", "tensor_tensor", g3[:], g2[:], g3[:], ALU.mult)
            O("dve", "reduce_sum", gs1[:], g3[:], axis=AX.X)
            O("dve", "reciprocal", gs2[:], gs1[:])
            O("dve", "tensor_single_scalar", wtok[:], g3[:], gs2[:], ALU.mult,
              incs=[(eGW, 1)])
            bk2 = 2 + sc % 2
            em.acquire("pe", ("PB", bk2))
            W("pe", eGW)
            O("pe", "transpose", PB[bk2][0:7, 0:128], wtok[:], idt[:], incs=[(eWT, 1)])
            W("dve", eWT)
            O("dve", "tensor_copy", wTt[:, sc * 128:(sc + 1) * 128], PB[bk2][0:7, 0:128],
              incs=[(eWS, 1)])
            em.release(("PB", bk2), eWS)
        ws_all = eWS.n
        if dbg and PH >= 4:
            W("sp", eWS, ws_all)
            O("sp", "dma_start", out=dbgw[s], in_=wTt[:], incs=[(cDB, 16)])
            W("dve", cDB)
        if PH < 5:
            for c in range(NCH):
                em.release(("buf", stri, c), cDB if dbg else eWS)
                em.release(("buf", ftsi, c), cDB if dbg else eWS)
            continue

        acc_rdy = {}
        for e in range(8):
            routed = e > 0
            if routed:
                for st in range(NST):
                    bk = 2 + st % 2
                    em.acquire("pe", ("PB", bk))
                    W("pe", eWS, ws_all)
                    em.acquire("pe", ("wbt", st))
                    O("pe", "matmul", PB[bk][:, :], ind7t[:, (e - 1) * 128:e * 128],
                      wTt[:, st * TS:(st + 1) * TS], start=True, stop=True,
                      skip_group_check=True, incs=[(eWB, 1)])
                    W("dve", eWB)
                    em.acquire("dve", ("wbt", st))
                    O("dve", "tensor_copy", wbt[:, st * TS:(st + 1) * TS], PB[bk][:, :],
                      incs=[(eWBS, 1)])
                    em.release(("PB", bk), eWBS)
            mats = (w1[e - 1], wg[e - 1], w2[e - 1]) if routed else (ws1, wsg, ws2)
            for m in range(16):
                parA = str(m % 2)
                if m == 0 and e > 0:
                    pass
                else:
                    em.acquire("pool", ("wpA", parA))
                    O("pool", "dma_start", out=wpA[m % 2][:], in_=mats[0][m],
                      incs=[(cW["A" + parA], 16)])
                    em.acquire("pool", ("wpB", parA))
                    O("pool", "dma_start", out=wpB[m % 2][:], in_=mats[1][m],
                      incs=[(cW["B" + parA], 16)])
                W("pe", cW["A" + parA])
                W("pe", cW["B" + parA])
                for st in range(NST):
                    b1 = st % 2
                    b2 = 2 + st % 2
                    em.acquire("pe", ("PB", b1))
                    for kc in range(16):
                        O("pe", "matmul", PB[b1][:, :], wpA[m % 2][:, kc * 128:(kc + 1) * 128],
                          s1t(kc, st), start=(kc == 0), stop=(kc == 15),
                          skip_group_check=True, incs=[(eH1, 1)] if kc == 15 else [])
                    W("act", eH1)
                    em.acquire("act", ("stmp", st % 2))
                    O("act", "activation", stmp[st % 2][:], PB[b1][:, :], AF.Silu,
                      bias=0.0, scale=1.0, incs=[(eSI, 1)])
                    em.release(("PB", b1), eSI)
                    em.acquire("pe", ("PB", b2))
                    for kc in range(16):
                        O("pe", "matmul", PB[b2][:, :], wpB[m % 2][:, kc * 128:(kc + 1) * 128],
                          s1t(kc, st), start=(kc == 0), stop=(kc == 15),
                          skip_group_check=True, incs=[(eHG, 1)] if kc == 15 else [])
                    W("dve", eSI)
                    W("dve", eHG)
                    if routed:
                        W("dve", eWBS)
                        em.acquire("dve", ("esb", st % 2))
                        O("dve", "tensor_tensor", esb[st % 2][:], stmp[st % 2][:],
                          PB[b2][:, :], ALU.mult)
                        O("dve", "tensor_tensor", s2t(m, st), esb[st % 2][:],
                          wbt[:, st * TS:(st + 1) * TS], ALU.mult, incs=[(eAP, 1)])
                        em.release(("esb", st % 2), eAP)
                    else:
                        O("dve", "tensor_tensor", s2t(m, st), stmp[st % 2][:],
                          PB[b2][:, :], ALU.mult, incs=[(eAP, 1)])
                    em.release(("PB", b2), eAP)
                    em.release(("stmp", st % 2), eAP)
                em.release(("wpA", parA), eH1)
                em.release(("wpB", parA), eHG)
            ap_all = eAP.n
            for st in range(NST):
                em.release(("wbt", st), (eAP, ap_all))
            nxt = (w1[e], wg[e]) if e < 7 else None
            last = e == 7
            outer = list(range(NST)) if last else [None]
            pcnt = [0]

            def _w2_panels(mp, st_):
                parA = str(pcnt[0] % 2)
                pcnt[0] += 1
                em.acquire("pool", ("wpA", parA))
                O("pool", "dma_start", out=wpA[int(parA)][:], in_=mats[2][2 * mp],
                  incs=[(cW["A" + parA], 16)])
                em.acquire("pool", ("wpB", parA))
                O("pool", "dma_start", out=wpB[int(parA)][:], in_=mats[2][2 * mp + 1],
                  incs=[(cW["B" + parA], 16)])
                return parA

            def _w2_unit(mp, st, parA):
                wpa, wpb = wpA[int(parA)], wpB[int(parA)]
                W("pe", cW["A" + parA])
                W("pe", cW["B" + parA])
                W("pe", eAP, ap_all)
                for half, wp in ((0, wpa), (1, wpb)):
                    spar = (mp * 2 + st * 2 + half) % 2
                    bk = half
                    em.acquire("pe", ("PB", bk))
                    for kc in range(16):
                        O("pe", "matmul", PB[bk][:, :], wp[:, kc * 128:(kc + 1) * 128],
                          s2t(kc, st), start=(kc == 0), stop=(kc == 15),
                          skip_group_check=True, incs=[(eW2, 1)] if kc == 15 else [])
                    W("dve", eW2)
                    em.acquire("dve", ("stage", spar))
                    O("dve", "tensor_copy", stage[spar][:], PB[bk][:, :],
                      incs=[(eSG, 1)])
                    em.release(("PB", bk), eSG)
                    W("pool", eSG)
                    dst = acch[s][2 * mp + half, :, st * TS:(st + 1) * TS]
                    kw = {} if e == 0 else {"accum_op": ALU.add}
                    O("pool", "dma_start", out=dst, in_=stage[spar][:],
                      incs=[(cAS[spar], 16)], **kw)
                    em.release(("stage", spar), cAS[spar])
                em.release(("wpA", parA), eW2)
                em.release(("wpB", parA), eW2)

            if not last:
                par_next = _w2_panels(0, None)
                for mp in range(8):
                    par_cur = par_next
                    if mp < 7:
                        par_next = _w2_panels(mp + 1, None)
                    else:
                        parA = str(pcnt[0] % 2)
                        em.acquire("pool", ("wpA", parA))
                        O("pool", "dma_start", out=wpA[int(parA)][:], in_=nxt[0][0],
                          incs=[(cW["A" + parA], 16)])
                        em.acquire("pool", ("wpB", parA))
                        O("pool", "dma_start", out=wpB[int(parA)][:], in_=nxt[1][0],
                          incs=[(cW["B" + parA], 16)])
                    for st in range(NST):
                        _w2_unit(mp, st, par_cur)
            else:
                par_next = _w2_panels(0, 0)
                for st in range(NST):
                    for mp in range(8):
                        par_cur = par_next
                        if not (st == NST - 1 and mp == 7):
                            par_next = _w2_panels((mp + 1) % 8, st if mp < 7 else st + 1)
                        _w2_unit(mp, st, par_cur)
                    acc_rdy[st] = (cAS[0].n, cAS[1].n)
                    emit_p6_st(s, stri, s1t, acc_rdy, st)
        for c in range(NCH):
            em.release(("buf", ftsi, c), eW2)



    W("sp", cOS[0])
    W("sp", cOS[1])
    if dbg:
        W("sp", cDB)
        W("sp", cAS[0])
        W("sp", cAS[1])
        W("sp", cSP[0])
        W("sp", cSP[1])

    with nc.Block() as block:
        @block.tensor
        def _(t):
            em.replay(t, em.q["pe"])

        @block.scalar
        def _(a):
            em.replay(a, em.q["act"])

        @block.vector
        def _(v):
            em.replay(v, em.q["dve"])

        @block.sync
        def _(sy):
            em.replay(sy, em.q["sp"])

        @block.gpsimd
        def _(g):
            em.replay(g, em.q["pool"])

    ctx.close()
    for h in reversed(sems):
        h.__exit__(None, None, None)
    ninstr = sum(len(v) for v in em.q.values())
    return nc, ninstr


def _pan(w):
    w = np.asarray(w, dtype=np.float32)
    return np.ascontiguousarray(
        w.reshape(16, 128, 16, 128).transpose(2, 1, 0, 3).reshape(16, 128, 2048))


def _prep_inputs(inputs):
    g = {k: np.asarray(v) for k, v in inputs.items()}
    shared = {
        "wq": _pan(g["L0_attn_Wq"]), "wk": _pan(g["L0_attn_Wk"]),
        "wv": _pan(g["L0_attn_Wv"]), "wo": _pan(g["L0_attn_Wo"]),
        "ws1": _pan(g["L0_ffn_Ws1"]), "wsg": _pan(g["L0_ffn_Wsg"]),
        "ws2": _pan(g["L0_ffn_Ws2"]),
        "w1": np.stack([_pan(g["L0_ffn_W1"][e]) for e in range(7)]),
        "wg": np.stack([_pan(g["L0_ffn_Wg"][e]) for e in range(7)]),
        "w2": np.stack([_pan(g["L0_ffn_W2"][e]) for e in range(7)]),
    }
    shared["gw"] = np.ascontiguousarray(
        np.asarray(g["L0_ffn_gate_w"], np.float32).reshape(16, 128, 7)
        .transpose(1, 0, 2).reshape(128, 112))
    bp = np.zeros((128, 80), np.float32)
    bp[:, 0:16] = np.asarray(g["L0_attn_bq"], np.float32).reshape(16, 128).T
    bp[:, 16:32] = np.asarray(g["L0_attn_bk"], np.float32).reshape(16, 128).T
    bp[:, 32:48] = np.asarray(g["L0_attn_bv"], np.float32).reshape(16, 128).T
    bp[:, 48:64] = np.asarray(g["L0_attn_bo"], np.float32).reshape(16, 128).T
    bp[0, 64:71] = np.asarray(g["L0_ffn_gate_b"], np.float32)
    bp[0, 71:78] = np.asarray(g["L0_ffn_biases"], np.float32)
    bp[:, 78] = 1.0e-5
    shared["bp"] = bp
    x = np.asarray(g["L0_attn_x"], np.float32)
    in_maps = []
    for c in range(8):
        m = dict(shared)
        m["xT"] = np.ascontiguousarray(x[2 * c:2 * c + 2].transpose(0, 2, 1))
        in_maps.append(m)
    return in_maps


def kernel(**inputs):
    in_maps = _prep_inputs(inputs)
    nc, _ = build_nc()
    res = run_bass_kernel_spmd(nc, in_maps, core_ids=list(range(8)))
    out = np.empty((16, 2048, 2048), np.float32)
    for c in range(8):
        o = np.asarray(res.results[c]["out"])
        out[2 * c:2 * c + 2] = o.reshape(2, 2048, 2048).transpose(0, 2, 1)
    return out


# version 11
# speedup vs baseline: 1.0463x; 1.0463x over previous
import math
import contextlib
import numpy as np
import ml_dtypes
import concourse.bass as bass
import concourse.mybir as mybir
from concourse.bass_utils import run_bass_kernel_spmd

F32 = mybir.dt.float32
BF16 = mybir.dt.bfloat16
AF = mybir.ActivationFunctionType
AX = mybir.AxisListType
ALU = mybir.AluOpType
BF = ml_dtypes.bfloat16

D = 2048
S = 2048
NSEQ = 2
NCH = 16
NST = 4
TS = 512
NHC = 16
NEXP = 7
BIG = 1.0e9
ISCALE = 1.0 / math.sqrt(32.0)
EPS = 1.0e-5


class Ch:
    __slots__ = ("sem", "n", "name")

    def __init__(self, sem, name):
        self.sem = sem
        self.n = 0
        self.name = name


class Em:
    ENGS = ("pe", "act", "dve", "sp", "pool")

    def __init__(self):
        self.q = {e: [] for e in self.ENGS}
        self.wcache = {e: {} for e in self.ENGS}
        self.gates = {}
        self.autochain = {}

    def wait(self, eng, ch, val=None):
        val = ch.n if val is None else val
        assert val <= ch.n, f"wait on future inc of {ch.name}: {val} > {ch.n}"
        if val <= 0:
            return
        c = self.wcache[eng]
        if c.get(id(ch), -1) >= val:
            return
        c[id(ch)] = val
        self.q[eng].append(("w", ch.sem, val))

    def op(self, eng, meth, *args, incs=(), **kw):
        ac = self.autochain.get(eng)
        if ac is not None:
            self.wait(eng, ac, ac.n)
            if not incs:
                incs = ((ac, 1),)
            else:
                assert len(incs) == 1 and incs[0][0] is ac, \
                    f"{eng} op must inc only the self-channel"
        lst = []
        for ch, k in incs:
            ch.n += k
            lst.append((ch.sem, k))
        self.q[eng].append(("o", meth, args, kw, lst))

    def acquire(self, eng, key):
        for ch, val in self.gates.get(key, ()):
            self.wait(eng, ch, val)

    def release(self, key, *pairs):
        out = []
        for p in pairs:
            if isinstance(p, tuple):
                out.append((p[0], p[1] if p[1] is not None else p[0].n))
            else:
                out.append((p, p.n))
        self.gates[key] = out

    def replay(self, eng, lst):
        for it in lst:
            if it[0] == "w":
                eng.wait_ge(it[1], it[2])
            else:
                _, meth, args, kw, incs = it
                ins = getattr(eng, meth)(*args, **kw)
                for sem, k in incs:
                    ins = ins.then_inc(sem, k)


def build_nc(upto="all", dbg=False, nmats=3, nmch=16, nseq=NSEQ):
    PH = {"p0": 0, "p1": 1, "p2": 2, "p3": 3, "p4": 4, "p5": 5, "all": 6}[upto]
    nc = bass.Bass("TRN2", target_bir_lowering=False,
                   dynamic_dma_scratch_size=16384)

    dt_i = "ExternalOutput" if dbg else "Internal"
    xT = nc.dram_tensor("xT", [NSEQ, D, S], F32, kind="ExternalInput").ap()
    wq = nc.dram_tensor("wq", [16, 128, D], F32, kind="ExternalInput").ap()
    wk = nc.dram_tensor("wk", [16, 128, D], F32, kind="ExternalInput").ap()
    wv = nc.dram_tensor("wv", [16, 128, D], F32, kind="ExternalInput").ap()
    wo = nc.dram_tensor("wo", [16, 128, D], F32, kind="ExternalInput").ap()
    w1 = nc.dram_tensor("w1", [NEXP, 16, 128, D], F32, kind="ExternalInput").ap()
    wg = nc.dram_tensor("wg", [NEXP, 16, 128, D], F32, kind="ExternalInput").ap()
    w2 = nc.dram_tensor("w2", [NEXP, 16, 128, D], F32, kind="ExternalInput").ap()
    ws1 = nc.dram_tensor("ws1", [16, 128, D], F32, kind="ExternalInput").ap()
    wsg = nc.dram_tensor("wsg", [16, 128, D], F32, kind="ExternalInput").ap()
    ws2 = nc.dram_tensor("ws2", [16, 128, D], F32, kind="ExternalInput").ap()
    gw = nc.dram_tensor("gw", [128, 112], F32, kind="ExternalInput").ap()
    bp = nc.dram_tensor("bp", [128, 80], F32, kind="ExternalInput").ap()
    out = nc.dram_tensor("out", [NSEQ, 16, 128, S], F32, kind="ExternalOutput").ap()
    qh = nc.dram_tensor("qh", [NSEQ, 16, 128, S], BF16, kind=dt_i).ap()
    kh = nc.dram_tensor("kh", [NSEQ, 16, 128, S], BF16, kind=dt_i).ap()
    vh = nc.dram_tensor("vh", [NSEQ, 16, 128, S], BF16, kind=dt_i).ap()
    acch = nc.dram_tensor("acch", [NSEQ, 16, 128, S], F32,
                          kind=("ExternalOutput" if (dbg and PH >= 5) else "Internal")).ap()
    dbgf = dbgy = dbgw = None
    if dbg and PH >= 2:
        dbgf = nc.dram_tensor("dbgf", [NSEQ, 16, 128, S], BF16, kind="ExternalOutput").ap()
    if dbg and PH >= 3:
        dbgy = nc.dram_tensor("dbgy", [NSEQ, 16, 128, S], BF16, kind="ExternalOutput").ap()
    if dbg and PH >= 4:
        dbgw = nc.dram_tensor("dbgw", [NSEQ, 7, S], BF16, kind="ExternalOutput").ap()

    ones_np = np.ones((128, 128), dtype=BF)
    mab_np = np.zeros((128, 128), dtype=BF)
    mab_np[0:64, 0:64] = 1
    mab_np[64:128, 64:128] = 1
    id_np = np.eye(128, dtype=np.float32)
    ind_np = np.zeros((7, 7 * 128), dtype=BF)
    for e in range(7):
        ind_np[e, e * 128:(e + 1) * 128] = 1
    c_ones = nc.inline_tensor(ones_np, "c_ones").ap()
    c_mab = nc.inline_tensor(mab_np, "c_mab").ap()
    c_id = nc.inline_tensor(id_np, "c_id").ap()
    c_ind = nc.inline_tensor(ind_np, "c_ind").ap()

    em = Em()
    sems = []

    def mkch(name):
        h = nc.semaphore(name)
        sem = h.__enter__()
        sems.append(h)
        return Ch(sem, name)

    ctx = contextlib.ExitStack()

    def sb(name, shape, dt):
        return ctx.enter_context(nc.sbuf_tensor(name, shape, dt))

    def pst(name, shape):
        return ctx.enter_context(nc.psum_tensor(name, shape, F32))

    S1 = sb("S1", [128, NCH * S], BF16)
    S2 = sb("S2", [128, NCH * S], BF16)
    expS = sb("expS", [128, 3072], BF16)
    qhc = [sb(f"qhc{i}", [128, S], BF16) for i in range(2)]
    khc = [sb(f"khc{i}", [128, S], BF16) for i in range(2)]
    vsb = [sb(f"vsb{i}", [128, S], BF16) for i in range(2)]
    wpA = [sb(f"wpA{i}", [128, D], BF16) for i in range(2)]
    wpB = [sb(f"wpB{i}", [128, D], BF16) for i in range(2)]
    gwt = sb("gwt", [128, 112], BF16)
    bpt = sb("bpt", [128, 80], F32)
    onest = sb("onest", [128, 128], BF16)
    mabt = sb("mabt", [128, 128], BF16)
    idt = sb("idt", [128, 128], F32)
    ind7t = sb("ind7t", [7, 896], BF16)
    esb = [sb(f"esb{i}", [128, TS], BF16) for i in range(2)]
    stmp = [sb(f"stmp{i}", [128, TS], F32) for i in range(2)]
    recipb = [sb(f"recip{i}", [128, TS], F32) for i in range(2)]
    meanb = sb("meanb", [128, TS], F32)
    rstdb = sb("rstdb", [128, TS], F32)
    stage = [sb(f"stage{i}", [128, TS], F32) for i in range(2)]
    astg = recipb
    ostg = [stage[0][:], stage[1][:]]
    wTt = sb("wTt", [7, S], BF16)
    wbt = sb("wbt", [128, S], BF16)
    gbb = sb("gbb", [128, 7], F32)
    bbb = sb("bbb", [128, 7], F32)
    grow = sb("grow", [1, 7], BF16)
    brow = sb("brow", [1, 7], BF16)
    g1 = sb("g1", [128, 7], F32)
    g2 = sb("g2", [128, 7], F32)
    g3 = sb("g3", [128, 7], F32)
    g4 = sb("g4", [128, 7], F32)
    gs1 = sb("gs1", [128, 1], F32)
    gs2 = sb("gs2", [128, 1], F32)
    wtok = sb("wtok", [128, 7], F32)
    tr1 = sb("tr1", [128, 1], F32)
    qn2a = sb("qn2a", [128, 16], F32)
    kn2a = sb("kn2a", [128, 16], F32)
    pm16 = sb("pm16", [128, 16], F32)
    c16 = sb("c16", [16, 1], F32)
    crow = sb("crow", [1, 16], BF16)
    cba = sb("cba", [128, 16], F32)

    PA = [pst("PA0", [128, 1024]), pst("PA1", [128, 1024])]
    PB = [pst(f"PB{i}", [128, TS]) for i in range(4)]

    cXL = mkch("cXL")
    cW = {"A0": mkch("cWA0"), "A1": mkch("cWA1"), "B0": mkch("cWB0"), "B1": mkch("cWB1")}
    cSP = [mkch("cSP0"), mkch("cSP1")]
    cHL = [mkch("cHL0"), mkch("cHL1")]
    cAS = [mkch("cAS0"), mkch("cAS1")]
    cAL = [mkch("cAL0"), mkch("cAL1")]
    cOS = [mkch("cOS0"), mkch("cOS1")]
    cMS = mkch("cMS")
    cDB = mkch("cDB")
    eAC = mkch("eAC")
    eDV = mkch("eDV")
    em.autochain = {"act": eAC, "dve": eDV}
    eEV = eEX = eSQ = eSI = eAC
    eCd = eRD = eNM = eFE = eGW = eWS = eWBS = eAP = eSG = eR2 = eY2 = eDV
    ePQ = mkch("ePQ")
    eSC = mkch("eSC")
    eAV = mkch("eAV")
    eHC = mkch("eHC")
    eC = mkch("eC")
    eAT = mkch("eAT")
    eSM = mkch("eSM")
    eGL = mkch("eGL")
    eWT = mkch("eWT")
    eWB = mkch("eWB")
    eH1 = mkch("eH1")
    eHG = mkch("eHG")
    eW2 = mkch("eW2")
    eP0 = mkch("eP0")
    ePS = mkch("ePS")

    W = em.wait
    O = em.op

    for dst, src in ((onest, c_ones), (mabt, c_mab), (idt, c_id), (ind7t, c_ind),
                     (gwt, gw), (bpt, bp), (grow, bp[0:1, 64:71]), (brow, bp[0:1, 71:78])):
        O("pool", "dma_start", out=dst[:], in_=src, incs=[(cMS, 16)])
    W("pe", cMS)
    O("pe", "matmul", PB[0][:, 0:7], onest[0:1, 0:128], grow[:], start=True, stop=True,
      skip_group_check=True, incs=[(eP0, 1)])
    O("pe", "matmul", PB[1][:, 0:7], onest[0:1, 0:128], brow[:], start=True, stop=True,
      skip_group_check=True, incs=[(eP0, 1)])
    W("dve", eP0)
    O("dve", "tensor_copy", gbb[:], PB[0][:, 0:7], incs=[(eCd, 1)])
    O("dve", "tensor_copy", bbb[:], PB[1][:, 0:7], incs=[(eCd, 1)])
    em.release(("PB", 0), eCd)
    em.release(("PB", 1), eCd)

    def ln_stats_finalize():
        W("act", eDV)
        W("dve", eSM)
        O("dve", "tensor_scalar_mul", meanb[:], PB[2][:, :], 1.0 / 2048.0)
        O("dve", "tensor_tensor", rstdb[:], meanb[:], meanb[:], ALU.mult)
        O("dve", "scalar_tensor_tensor", rstdb[:], PB[3][:, :], 1.0 / 2048.0, rstdb[:],
          ALU.mult, ALU.subtract)
        em.release(("PB", 2), eDV)
        em.release(("PB", 3), eDV)
        W("act", eDV)
        O("act", "activation", rstdb[:], rstdb[:], AF.Ln, bias=bpt[:, 78:79], scale=1.0)
        O("act", "activation", rstdb[:], rstdb[:], AF.Exp, bias=0.0, scale=-0.5)
        return eAC.n

    def ln_stats_pass(getr, rdy=None):
        em.acquire("pe", ("PB", 2))
        em.acquire("pe", ("PB", 3))
        for m in range(16):
            epar = m % 2
            if rdy is not None:
                W("act", rdy[m][0], rdy[m][1])
            em.acquire("act", ("esb", epar))
            O("act", "activation", esb[epar][:], getr(m), AF.Square, incs=[(eSQ, 1)])
            W("pe", eSQ)
            O("pe", "matmul", PB[2][:, :], onest[:], getr(m), start=(m == 0),
              stop=(m == 15), skip_group_check=True)
            O("pe", "matmul", PB[3][:, :], onest[:], esb[epar][:], start=(m == 0),
              stop=(m == 15), skip_group_check=True, incs=[(eSM, 1)])
            em.release(("esb", epar), eSM)
        return ln_stats_finalize()

    def emit_p6_st(ps, pstri, ps1t, prdy, st):
        W("sp", cAS[0], prdy[st][0])
        W("sp", cAS[1], prdy[st][1])
        rdy = []
        for m in range(16):
            apar = m % 2
            em.acquire("sp", ("rbuf", apar))
            O("sp", "dma_start", out=astg[apar][:],
              in_=acch[ps, m, :, st * TS:(st + 1) * TS], incs=[(cAL[apar], 16)])
            W("dve", cAL[apar])
            O("dve", "tensor_tensor", ps1t(m, st), astg[apar][:], ps1t(m, st),
              ALU.add)
            rdy.append((eDV, eDV.n))
            em.release(("rbuf", apar), eDV)
        ssv = ln_stats_pass(lambda m: ps1t(m, st), rdy)
        for m in range(16):
            opar = m % 2
            W("dve", eAC, ssv)
            em.acquire("dve", ("stage", opar))
            O("dve", "tensor_tensor", stmp[0][:], ps1t(m, st), meanb[:], ALU.subtract)
            O("dve", "tensor_tensor", ostg[opar], stmp[0][:], rstdb[:], ALU.mult)
            W("sp", eY2)
            O("sp", "dma_start", out=out[ps, m, :, st * TS:(st + 1) * TS],
              in_=ostg[opar], incs=[(cOS[opar], 16)])
            em.release(("stage", opar), cOS[opar])
        if st == NST - 1:
            for c in range(NCH):
                em.release(("buf", pstri, c), eDV)

    for s in range(nseq if PH >= 1 else 0):
        stri = s % 2
        ftsi = 1 - stri
        STR = S1 if stri == 0 else S2
        FTS = S2 if stri == 0 else S1

        def s1t(c, st, _B=STR):
            a = c * S + st * TS
            return _B[:, a:a + TS]

        def s2t(c, st, _B=FTS):
            a = c * S + st * TS
            return _B[:, a:a + TS]

        for c in range(NCH):
            em.acquire("pool", ("buf", stri, c))
            O("pool", "dma_start", out=STR[:, c * S:(c + 1) * S],
              in_=xT[s, c * 128:(c + 1) * 128, :], incs=[(cXL, 16)])
        xl_done = cXL.n

        p1b = 0
        ep1 = 0
        np1 = 0
        nsq = [0]
        npend = []

        def flush_norm(k=1):
            nonlocal np1
            for _ in range(k):
                if not npend:
                    return
                sq, dst, first, sq_dv = npend.pop(0)
                pa_ = np1 % 2
                np1 += 1
                em.acquire("pe", ("PA", pa_))
                W("pe", eDV, sq_dv)
                O("pe", "matmul", PA[pa_][:, 0:512], mabt[:], sq, start=True,
                  stop=True, skip_group_check=True, incs=[(eC, 1)])
                em.release(("wbt", slot), eC)
                W("dve", eC)
                O("dve", "reduce_max", tr1[:], PA[pa_][:, 0:512], axis=AX.X)
                em.release(("PA", pa_), eDV)
                if first:
                    O("dve", "tensor_copy", dst, tr1[:])
                else:
                    O("dve", "tensor_max", dst, tr1[:], dst)

        trip = ((wq, qh, 0), (wk, kh, 16), (wv, vh, 32))
        spill_rdy = {}
        cb_rdy = {}
        qk_rdy = {}
        p1c = [0]

        def emit_cb(m):
            O("dve", "tensor_tensor", tr1[:], qn2a[:, m:m + 1], kn2a[:, m:m + 1],
              ALU.add)
            O("dve", "tensor_scalar_mul", tr1[:], tr1[:], -0.5 * ISCALE)
            em.acquire("pe", ("PA", 0))
            W("pe", eDV)
            O("pe", "transpose", PA[0][0:1, 0:128], tr1[:], idt[:], incs=[(eC, 1)])
            W("dve", eC)
            O("dve", "tensor_reduce", c16[0:1, 0:1], PA[0][0:1, 0:128], AX.X, ALU.min)
            em.release(("PA", 0), eDV)
            O("dve", "tensor_copy", crow[0:1, 0:1], c16[0:1, 0:1])
            em.acquire("pe", ("PA", 1))
            W("pe", eDV)
            O("pe", "matmul", PA[1][:, 0:1], onest[0:1, 0:128], crow[0:1, 0:1],
              start=True, stop=True, skip_group_check=True, incs=[(eC, 1)])
            W("dve", eC)
            O("dve", "tensor_copy", cba[:, m:m + 1], PA[1][:, 0:1])
            em.release(("PA", 1), eDV)

        def emit_p2_loads(hc):
            par = hc % 2
            W("sp", cSP[0], spill_rdy[hc][0])
            W("sp", cSP[1], spill_rdy[hc][1])
            em.acquire("sp", ("hcv", par))
            for j in range(16):
                O("sp", "dma_start_transpose", vsb[par][:, j * 128:(j + 1) * 128],
                  vh[s, hc, :, j * 128:(j + 1) * 128], incs=[(cHL[par], 16)])
            return cHL[par].n

        def p2_gen(hc, hl_done):
            par = hc % 2
            W("pe", cHL[par], hl_done)
            W("pe", eDV, qk_rdy[hc])
            W("act", eDV, cb_rdy[hc])
            pend = [None]

            def emit_rhoav(it):
                st_, j_, slot_, exn = it
                pr, pf = 2, 3
                W("pe", eEX, exn)
                if j_ == 0:
                    em.acquire("pe", ("PB", pr))
                    em.acquire("pe", ("PB", pf))
                eslot = expS[:, slot_ * 1024:(slot_ + 1) * 1024]
                O("pe", "matmul", PB[pr][0:64, :], onest[:, 0:64], eslot[:, 0:512],
                  start=(j_ == 0), stop=(j_ == 15), skip_group_check=True,
                  tile_position=(0, 0))
                O("pe", "matmul", PB[pr][64:128, :], onest[:, 64:128], eslot[:, 512:1024],
                  start=(j_ == 0), stop=(j_ == 15), skip_group_check=True,
                  tile_position=(0, 64))
                O("pe", "matmul", PB[pf][0:64, :], vsb[par][:, j_ * 128:j_ * 128 + 64],
                  eslot[:, 0:512], start=(j_ == 0), stop=(j_ == 15),
                  skip_group_check=True, tile_position=(0, 0), incs=[(eAV, 1)])
                O("pe", "matmul", PB[pf][64:128, :], vsb[par][:, j_ * 128 + 64:(j_ + 1) * 128],
                  eslot[:, 512:1024], start=(j_ == 0), stop=(j_ == 15),
                  skip_group_check=True, tile_position=(0, 64), incs=[(eAV, 1)])
                em.release(("expS", slot_), eAV)
                if j_ == 15:
                    st2par = st_ % 2
                    W("dve", eAV)
                    em.acquire("dve", ("rbuf", st2par))
                    O("dve", "reciprocal", recipb[st2par][:], PB[pr][:, :])
                    O("dve", "tensor_tensor", s2t(hc, st_), PB[pf][:, :],
                      recipb[st2par][:], ALU.mult, incs=[(eFE, 1)])
                    em.release(("rbuf", st2par), eFE)
                    em.release(("PB", pr), eFE)
                    em.release(("PB", pf), eFE)

            glb = 0
            for st in range(NST):
                for j in range(16):
                    slot = glb % 3
                    pa = glb % 2
                    glb += 1
                    em.acquire("pe", ("PA", pa))
                    O("pe", "matmul", PA[pa][:, 0:512], khc[par][0:64, j * 128:(j + 1) * 128],
                      qhc[par][0:64, st * TS:(st + 1) * TS], start=True, stop=True,
                      skip_group_check=True, tile_position=(0, 0))
                    O("pe", "matmul", PA[pa][:, 512:1024],
                      khc[par][64:128, j * 128:(j + 1) * 128],
                      qhc[par][64:128, st * TS:(st + 1) * TS], start=True, stop=True,
                      skip_group_check=True, tile_position=(64, 0), incs=[(eSC, 1)])
                    W("act", eSC)
                    em.acquire("act", ("expS", slot))
                    O("act", "activation", expS[:, slot * 1024:(slot + 1) * 1024],
                      PA[pa][:, :], AF.Exp, bias=cba[:, hc:hc + 1], scale=ISCALE,
                      incs=[(eEX, 1)])
                    em.release(("PA", pa), eEX)
                    if pend[0] is not None:
                        emit_rhoav(pend[0])
                    pend[0] = (st, j, slot, eEX.n)
                if st == NST - 1:
                    emit_rhoav(pend[0])
                    em.release(("hcq", par), eSC)
                    em.release(("hcv", par), eAV)
                yield

        for m in range(nmch):
            gen = None
            if PH >= 2 and m >= 1 and nmats == 3:
                gen = p2_gen(m - 1, emit_p2_loads(m - 1))
            gi = 0
            hcpar = m % 2
            for mat, spill, bcol in trip[:nmats]:
                par = str(p1c[0] % 2)
                wb_ = wpA[p1c[0] % 2]
                p1c[0] += 1
                em.acquire("pool", ("wpA", par))
                O("pool", "dma_start", out=wb_[:], in_=mat[m],
                  incs=[(cW["A" + par], 16)])
                W("pe", cW["A" + par])
                W("pe", cXL, xl_done)
                if mat is wq:
                    em.acquire("dve", ("hcq", hcpar))
                for st in range(NST):
                    bk = p1b % 2
                    p1b += 1
                    em.acquire("pe", ("PB", bk))
                    for kc in range(16):
                        O("pe", "matmul", PB[bk][:, :], wb_[:, kc * 128:(kc + 1) * 128],
                          s1t(kc, st), start=(kc == 0), stop=(kc == 15),
                          skip_group_check=True, incs=[(ePQ, 1)] if kc == 15 else [])
                    W("dve", ePQ)
                    if mat is wv:
                        epar = ep1 % 2
                        ep1 += 1
                        em.acquire("dve", ("esb", epar))
                        O("dve", "tensor_scalar_add", esb[epar][:], PB[bk][:, :],
                          bpt[:, bcol + m:bcol + m + 1])
                        evac_dv = eDV.n
                        em.release(("PB", bk), eDV)
                        W("sp", eDV, evac_dv)
                        O("sp", "dma_start", out=spill[s, m, :, st * TS:(st + 1) * TS],
                          in_=esb[epar][:], incs=[(cSP[epar], 16)])
                        em.release(("esb", epar), cSP[epar])
                    else:
                        dstb = qhc[hcpar] if mat is wq else khc[hcpar]
                        O("dve", "tensor_scalar_add", dstb[:, st * TS:(st + 1) * TS],
                          PB[bk][:, :], bpt[:, bcol + m:bcol + m + 1])
                        evac_dv = eDV.n
                        em.release(("PB", bk), eDV)
                        if dbg:
                            W("sp", eDV, evac_dv)
                            O("sp", "dma_start",
                              out=spill[s, m, :, st * TS:(st + 1) * TS],
                              in_=dstb[:, st * TS:(st + 1) * TS], incs=[(cSP[0], 16)])
                        slot = nsq[0] % 4
                        nsq[0] += 1
                        em.acquire("dve", ("wbt", slot))
                        sq = wbt[:, slot * 512:(slot + 1) * 512]
                        O("dve", "tensor_tensor", sq,
                          dstb[:, st * TS:(st + 1) * TS],
                          dstb[:, st * TS:(st + 1) * TS], ALU.mult)
                        npend.append((sq, (qn2a if mat is wq else kn2a)[:, m:m + 1],
                                      st == 0, eDV.n))
                    flush_norm()
                    gi += 1
                    if gen is not None and gi % 3 == 0:
                        next(gen)
                em.release(("wpA", par), ePQ)
                if mat is wk:
                    qk_rdy[m] = eDV.n
            if nmats == 3 and PH >= 2:
                flush_norm(len(npend))
                emit_cb(m)
                cb_rdy[m] = eDV.n
            spill_rdy[m] = (cSP[0].n, cSP[1].n)
        flush_norm(len(npend))

        sp_q0 = cSP[0].n
        sp_q1 = cSP[1].n
        if PH < 2:
            for c in range(NCH):
                em.release(("buf", stri, c), ePQ)
            continue
        gen = p2_gen(nmch - 1, emit_p2_loads(nmch - 1))
        for _ in range(NST):
            next(gen)

        fe_all = eFE.n
        if dbg and PH >= 2:
            W("sp", eFE, fe_all)
            for c in range(NCH):
                O("sp", "dma_start", out=dbgf[s, c], in_=FTS[:, c * S:(c + 1) * S],
                  incs=[(cDB, 16)])
            W("dve", cDB)
        if PH < 3:
            for c in range(NCH):
                em.release(("buf", stri, c), cDB if dbg else eFE)
                em.release(("buf", ftsi, c), cDB if dbg else eFE)
            continue

        p3b = 0
        for m in range(16):
            par = str(m % 2)
            em.acquire("pool", ("wpA", par))
            O("pool", "dma_start", out=wpA[m % 2][:], in_=wo[m],
              incs=[(cW["A" + par], 16)])
            W("pe", cW["A" + par])
            W("pe", eFE, fe_all)
            for st in range(NST):
                bk = p3b % 2
                p3b += 1
                em.acquire("pe", ("PB", bk))
                for kc in range(16):
                    O("pe", "matmul", PB[bk][:, :], wpA[m % 2][:, kc * 128:(kc + 1) * 128],
                      s2t(kc, st), start=(kc == 0), stop=(kc == 15),
                      skip_group_check=True, incs=[(eAT, 1)] if kc == 15 else [])
                W("dve", eAT)
                O("dve", "scalar_tensor_tensor", s1t(m, st), PB[bk][:, :],
                  bpt[:, 48 + m:48 + m + 1], s1t(m, st), ALU.add, ALU.add,
                  incs=[(eRD, 1)])
                em.release(("PB", bk), eRD)
            em.release(("wpA", par), eAT)
        for st in range(NST):
            W("act", eDV)
            ssv = ln_stats_pass(lambda m: s1t(m, st))
            for m in range(16):
                W("dve", eAC, ssv)
                O("dve", "tensor_tensor", stmp[0][:], s1t(m, st), meanb[:], ALU.subtract)
                O("dve", "tensor_tensor", s1t(m, st), stmp[0][:], rstdb[:], ALU.mult)
        nm_all = eNM.n
        if dbg and PH >= 3:
            W("sp", eNM, nm_all)
            for c in range(NCH):
                O("sp", "dma_start", out=dbgy[s, c], in_=STR[:, c * S:(c + 1) * S],
                  incs=[(cDB, 16)])
            W("dve", cDB)
        if PH < 4:
            for c in range(NCH):
                em.release(("buf", stri, c), cDB if dbg else eNM)
                em.release(("buf", ftsi, c), cDB if dbg else eNM)
            continue

        for sc in range(16):
            bk = sc % 2
            em.acquire("pe", ("PB", bk))
            W("pe", eNM, nm_all)
            for kc in range(16):
                O("pe", "matmul", PB[bk][:, 0:7],
                  STR[:, kc * S + sc * 128:kc * S + (sc + 1) * 128],
                  gwt[:, kc * 7:(kc + 1) * 7], start=(kc == 0), stop=(kc == 15),
                  skip_group_check=True, incs=[(eGL, 1)] if kc == 15 else [])
            W("dve", eGL)
            O("dve", "tensor_tensor", g1[:], PB[bk][:, 0:7], gbb[:], ALU.add,
              incs=[(eCd, 1)])
            em.release(("PB", bk), eCd)
            O("dve", "reduce_max", gs1[:], g1[:], axis=AX.X)
            O("dve", "tensor_single_scalar", g1[:], g1[:], gs1[:], ALU.subtract,
              incs=[(eCd, 1)])
            W("act", eCd)
            O("act", "activation", g2[:], g1[:], AF.Exp, bias=0.0, scale=1.0,
              incs=[(eSQ, 1)])
            W("dve", eSQ)
            O("dve", "reduce_sum", gs1[:], g2[:], axis=AX.X)
            O("dve", "reciprocal", gs2[:], gs1[:])
            O("dve", "tensor_single_scalar", g2[:], g2[:], gs2[:], ALU.mult)
            O("dve", "tensor_tensor", g4[:], g2[:], bbb[:], ALU.add)
            O("dve", "tensor_copy", g1[:], g4[:])
            for _ in range(2):
                O("dve", "reduce_max", gs1[:], g1[:], axis=AX.X)
                O("dve", "tensor_single_scalar", g3[:], g1[:], gs1[:], ALU.is_ge)
                O("dve", "scalar_tensor_tensor", g1[:], g3[:], -BIG, g1[:],
                  ALU.mult, ALU.add)
            O("dve", "reduce_max", gs1[:], g1[:], axis=AX.X)
            O("dve", "tensor_single_scalar", g3[:], g4[:], gs1[:], ALU.is_ge)
            O("dve", "tensor_tensor", g3[:], g2[:], g3[:], ALU.mult)
            O("dve", "reduce_sum", gs1[:], g3[:], axis=AX.X)
            O("dve", "reciprocal", gs2[:], gs1[:])
            O("dve", "tensor_single_scalar", wtok[:], g3[:], gs2[:], ALU.mult,
              incs=[(eGW, 1)])
            bk2 = 2 + sc % 2
            em.acquire("pe", ("PB", bk2))
            W("pe", eGW)
            O("pe", "transpose", PB[bk2][0:7, 0:128], wtok[:], idt[:], incs=[(eWT, 1)])
            W("dve", eWT)
            O("dve", "tensor_copy", wTt[:, sc * 128:(sc + 1) * 128], PB[bk2][0:7, 0:128],
              incs=[(eWS, 1)])
            em.release(("PB", bk2), eWS)
        ws_all = eWS.n
        if dbg and PH >= 4:
            W("sp", eWS, ws_all)
            O("sp", "dma_start", out=dbgw[s], in_=wTt[:], incs=[(cDB, 16)])
            W("dve", cDB)
        if PH < 5:
            for c in range(NCH):
                em.release(("buf", stri, c), cDB if dbg else eWS)
                em.release(("buf", ftsi, c), cDB if dbg else eWS)
            continue

        acc_rdy = {}
        for e in range(8):
            routed = e > 0
            if routed:
                for st in range(NST):
                    bk = 2 + st % 2
                    em.acquire("pe", ("PB", bk))
                    W("pe", eWS, ws_all)
                    em.acquire("pe", ("wbt", st))
                    O("pe", "matmul", PB[bk][:, :], ind7t[:, (e - 1) * 128:e * 128],
                      wTt[:, st * TS:(st + 1) * TS], start=True, stop=True,
                      skip_group_check=True, incs=[(eWB, 1)])
                    W("dve", eWB)
                    em.acquire("dve", ("wbt", st))
                    O("dve", "tensor_copy", wbt[:, st * TS:(st + 1) * TS], PB[bk][:, :],
                      incs=[(eWBS, 1)])
                    em.release(("PB", bk), eWBS)
            mats = (w1[e - 1], wg[e - 1], w2[e - 1]) if routed else (ws1, wsg, ws2)
            for m in range(16):
                parA = str(m % 2)
                if m == 0 and e > 0:
                    pass
                else:
                    em.acquire("pool", ("wpA", parA))
                    O("pool", "dma_start", out=wpA[m % 2][:], in_=mats[0][m],
                      incs=[(cW["A" + parA], 16)])
                    em.acquire("pool", ("wpB", parA))
                    O("pool", "dma_start", out=wpB[m % 2][:], in_=mats[1][m],
                      incs=[(cW["B" + parA], 16)])
                W("pe", cW["A" + parA])
                W("pe", cW["B" + parA])
                for st in range(NST):
                    b1 = st % 2
                    b2 = 2 + st % 2
                    em.acquire("pe", ("PB", b1))
                    for kc in range(16):
                        O("pe", "matmul", PB[b1][:, :], wpA[m % 2][:, kc * 128:(kc + 1) * 128],
                          s1t(kc, st), start=(kc == 0), stop=(kc == 15),
                          skip_group_check=True, incs=[(eH1, 1)] if kc == 15 else [])
                    W("act", eH1)
                    em.acquire("act", ("stmp", st % 2))
                    O("act", "activation", stmp[st % 2][:], PB[b1][:, :], AF.Silu,
                      bias=0.0, scale=1.0, incs=[(eSI, 1)])
                    em.release(("PB", b1), eSI)
                    em.acquire("pe", ("PB", b2))
                    for kc in range(16):
                        O("pe", "matmul", PB[b2][:, :], wpB[m % 2][:, kc * 128:(kc + 1) * 128],
                          s1t(kc, st), start=(kc == 0), stop=(kc == 15),
                          skip_group_check=True, incs=[(eHG, 1)] if kc == 15 else [])
                    W("dve", eSI)
                    W("dve", eHG)
                    if routed:
                        W("dve", eWBS)
                        em.acquire("dve", ("esb", st % 2))
                        O("dve", "tensor_tensor", esb[st % 2][:], stmp[st % 2][:],
                          PB[b2][:, :], ALU.mult)
                        O("dve", "tensor_tensor", s2t(m, st), esb[st % 2][:],
                          wbt[:, st * TS:(st + 1) * TS], ALU.mult, incs=[(eAP, 1)])
                        em.release(("esb", st % 2), eAP)
                    else:
                        O("dve", "tensor_tensor", s2t(m, st), stmp[st % 2][:],
                          PB[b2][:, :], ALU.mult, incs=[(eAP, 1)])
                    em.release(("PB", b2), eAP)
                    em.release(("stmp", st % 2), eAP)
                em.release(("wpA", parA), eH1)
                em.release(("wpB", parA), eHG)
            ap_all = eAP.n
            for st in range(NST):
                em.release(("wbt", st), (eAP, ap_all))
            nxt = (w1[e], wg[e]) if e < 7 else None
            last = e == 7
            outer = list(range(NST)) if last else [None]
            pcnt = [0]

            def _w2_panels(mp, st_):
                parA = str(pcnt[0] % 2)
                pcnt[0] += 1
                em.acquire("pool", ("wpA", parA))
                O("pool", "dma_start", out=wpA[int(parA)][:], in_=mats[2][2 * mp],
                  incs=[(cW["A" + parA], 16)])
                em.acquire("pool", ("wpB", parA))
                O("pool", "dma_start", out=wpB[int(parA)][:], in_=mats[2][2 * mp + 1],
                  incs=[(cW["B" + parA], 16)])
                return parA

            def _w2_unit(mp, st, parA):
                wpa, wpb = wpA[int(parA)], wpB[int(parA)]
                W("pe", cW["A" + parA])
                W("pe", cW["B" + parA])
                W("pe", eAP, ap_all)
                for half, wp in ((0, wpa), (1, wpb)):
                    spar = (mp * 2 + st * 2 + half) % 2
                    bk = half
                    em.acquire("pe", ("PB", bk))
                    for kc in range(16):
                        O("pe", "matmul", PB[bk][:, :], wp[:, kc * 128:(kc + 1) * 128],
                          s2t(kc, st), start=(kc == 0), stop=(kc == 15),
                          skip_group_check=True, incs=[(eW2, 1)] if kc == 15 else [])
                    W("dve", eW2)
                    em.acquire("dve", ("stage", spar))
                    O("dve", "tensor_copy", stage[spar][:], PB[bk][:, :],
                      incs=[(eSG, 1)])
                    em.release(("PB", bk), eSG)
                    W("pool", eSG)
                    dst = acch[s][2 * mp + half, :, st * TS:(st + 1) * TS]
                    kw = {} if e == 0 else {"accum_op": ALU.add}
                    O("pool", "dma_start", out=dst, in_=stage[spar][:],
                      incs=[(cAS[spar], 16)], **kw)
                    em.release(("stage", spar), cAS[spar])
                em.release(("wpA", parA), eW2)
                em.release(("wpB", parA), eW2)

            if not last:
                par_next = _w2_panels(0, None)
                for mp in range(8):
                    par_cur = par_next
                    if mp < 7:
                        par_next = _w2_panels(mp + 1, None)
                    else:
                        parA = str(pcnt[0] % 2)
                        em.acquire("pool", ("wpA", parA))
                        O("pool", "dma_start", out=wpA[int(parA)][:], in_=nxt[0][0],
                          incs=[(cW["A" + parA], 16)])
                        em.acquire("pool", ("wpB", parA))
                        O("pool", "dma_start", out=wpB[int(parA)][:], in_=nxt[1][0],
                          incs=[(cW["B" + parA], 16)])
                    for st in range(NST):
                        _w2_unit(mp, st, par_cur)
            else:
                par_next = _w2_panels(0, 0)
                for st in range(NST):
                    for mp in range(8):
                        par_cur = par_next
                        if not (st == NST - 1 and mp == 7):
                            par_next = _w2_panels((mp + 1) % 8, st if mp < 7 else st + 1)
                        _w2_unit(mp, st, par_cur)
                    acc_rdy[st] = (cAS[0].n, cAS[1].n)
                    emit_p6_st(s, stri, s1t, acc_rdy, st)
        for c in range(NCH):
            em.release(("buf", ftsi, c), eW2)



    W("sp", cOS[0])
    W("sp", cOS[1])
    if dbg:
        W("sp", cDB)
        W("sp", cAS[0])
        W("sp", cAS[1])
        W("sp", cSP[0])
        W("sp", cSP[1])

    with nc.Block() as block:
        @block.tensor
        def _(t):
            em.replay(t, em.q["pe"])

        @block.scalar
        def _(a):
            em.replay(a, em.q["act"])

        @block.vector
        def _(v):
            em.replay(v, em.q["dve"])

        @block.sync
        def _(sy):
            em.replay(sy, em.q["sp"])

        @block.gpsimd
        def _(g):
            em.replay(g, em.q["pool"])

    ctx.close()
    for h in reversed(sems):
        h.__exit__(None, None, None)
    ninstr = sum(len(v) for v in em.q.values())
    return nc, ninstr


def _pan(w):
    w = np.asarray(w, dtype=np.float32)
    return np.ascontiguousarray(
        w.reshape(16, 128, 16, 128).transpose(2, 1, 0, 3).reshape(16, 128, 2048))


def _prep_inputs(inputs):
    g = {k: np.asarray(v) for k, v in inputs.items()}
    shared = {
        "wq": _pan(g["L0_attn_Wq"]), "wk": _pan(g["L0_attn_Wk"]),
        "wv": _pan(g["L0_attn_Wv"]), "wo": _pan(g["L0_attn_Wo"]),
        "ws1": _pan(g["L0_ffn_Ws1"]), "wsg": _pan(g["L0_ffn_Wsg"]),
        "ws2": _pan(g["L0_ffn_Ws2"]),
        "w1": np.stack([_pan(g["L0_ffn_W1"][e]) for e in range(7)]),
        "wg": np.stack([_pan(g["L0_ffn_Wg"][e]) for e in range(7)]),
        "w2": np.stack([_pan(g["L0_ffn_W2"][e]) for e in range(7)]),
    }
    shared["gw"] = np.ascontiguousarray(
        np.asarray(g["L0_ffn_gate_w"], np.float32).reshape(16, 128, 7)
        .transpose(1, 0, 2).reshape(128, 112))
    bp = np.zeros((128, 80), np.float32)
    bp[:, 0:16] = np.asarray(g["L0_attn_bq"], np.float32).reshape(16, 128).T
    bp[:, 16:32] = np.asarray(g["L0_attn_bk"], np.float32).reshape(16, 128).T
    bp[:, 32:48] = np.asarray(g["L0_attn_bv"], np.float32).reshape(16, 128).T
    bp[:, 48:64] = np.asarray(g["L0_attn_bo"], np.float32).reshape(16, 128).T
    bp[0, 64:71] = np.asarray(g["L0_ffn_gate_b"], np.float32)
    bp[0, 71:78] = np.asarray(g["L0_ffn_biases"], np.float32)
    bp[:, 78] = 1.0e-5
    shared["bp"] = bp
    x = np.asarray(g["L0_attn_x"], np.float32)
    in_maps = []
    for c in range(8):
        m = dict(shared)
        m["xT"] = np.ascontiguousarray(x[2 * c:2 * c + 2].transpose(0, 2, 1))
        in_maps.append(m)
    return in_maps


def kernel(**inputs):
    in_maps = _prep_inputs(inputs)
    nc, _ = build_nc()
    res = run_bass_kernel_spmd(nc, in_maps, core_ids=list(range(8)))
    out = np.empty((16, 2048, 2048), np.float32)
    for c in range(8):
        o = np.asarray(res.results[c]["out"])
        out[2 * c:2 * c + 2] = o.reshape(2, 2048, 2048).transpose(0, 2, 1)
    return out


# version 13
# speedup vs baseline: 1.0588x; 1.0120x over previous
import math
import contextlib
import numpy as np
import ml_dtypes
import concourse.bass as bass
import concourse.mybir as mybir
from concourse.bass_utils import run_bass_kernel_spmd

F32 = mybir.dt.float32
BF16 = mybir.dt.bfloat16
AF = mybir.ActivationFunctionType
AX = mybir.AxisListType
ALU = mybir.AluOpType
BF = ml_dtypes.bfloat16

D = 2048
S = 2048
NSEQ = 2
NCH = 16
NST = 4
TS = 512
NHC = 16
NEXP = 7
BIG = 1.0e9
ISCALE = 1.0 / math.sqrt(32.0)
EPS = 1.0e-5


class Ch:
    __slots__ = ("sem", "n", "name")

    def __init__(self, sem, name):
        self.sem = sem
        self.n = 0
        self.name = name


class Em:
    ENGS = ("pe", "act", "dve", "sp", "pool")

    def __init__(self):
        self.q = {e: [] for e in self.ENGS}
        self.wcache = {e: {} for e in self.ENGS}
        self.gates = {}
        self.autochain = {}

    def wait(self, eng, ch, val=None):
        val = ch.n if val is None else val
        assert val <= ch.n, f"wait on future inc of {ch.name}: {val} > {ch.n}"
        if val <= 0:
            return
        c = self.wcache[eng]
        if c.get(id(ch), -1) >= val:
            return
        c[id(ch)] = val
        self.q[eng].append(("w", ch.sem, val))

    def op(self, eng, meth, *args, incs=(), **kw):
        ac = self.autochain.get(eng)
        if ac is not None:
            self.wait(eng, ac, ac.n)
            if not incs:
                incs = ((ac, 1),)
            else:
                assert len(incs) == 1 and incs[0][0] is ac, \
                    f"{eng} op must inc only the self-channel"
        lst = []
        for ch, k in incs:
            ch.n += k
            lst.append((ch.sem, k))
        self.q[eng].append(("o", meth, args, kw, lst))

    def acquire(self, eng, key):
        for ch, val in self.gates.get(key, ()):
            self.wait(eng, ch, val)

    def release(self, key, *pairs):
        out = []
        for p in pairs:
            if isinstance(p, tuple):
                out.append((p[0], p[1] if p[1] is not None else p[0].n))
            else:
                out.append((p, p.n))
        self.gates[key] = out

    def replay(self, eng, lst):
        for it in lst:
            if it[0] == "w":
                eng.wait_ge(it[1], it[2])
            else:
                _, meth, args, kw, incs = it
                ins = getattr(eng, meth)(*args, **kw)
                for sem, k in incs:
                    ins = ins.then_inc(sem, k)


def build_nc(upto="all", dbg=False, nmats=3, nmch=16, nseq=NSEQ):
    PH = {"p0": 0, "p1": 1, "p2": 2, "p3": 3, "p4": 4, "p5": 5, "all": 6}[upto]
    nc = bass.Bass("TRN2", target_bir_lowering=False,
                   dynamic_dma_scratch_size=16384)

    dt_i = "ExternalOutput" if dbg else "Internal"
    xT = nc.dram_tensor("xT", [NSEQ, D, S], F32, kind="ExternalInput").ap()
    wq = nc.dram_tensor("wq", [16, 128, D], F32, kind="ExternalInput").ap()
    wk = nc.dram_tensor("wk", [16, 128, D], F32, kind="ExternalInput").ap()
    wv = nc.dram_tensor("wv", [16, 128, D], F32, kind="ExternalInput").ap()
    wo = nc.dram_tensor("wo", [16, 128, D], F32, kind="ExternalInput").ap()
    w1 = nc.dram_tensor("w1", [NEXP, 16, 128, D], F32, kind="ExternalInput").ap()
    wg = nc.dram_tensor("wg", [NEXP, 16, 128, D], F32, kind="ExternalInput").ap()
    w2 = nc.dram_tensor("w2", [NEXP, 16, 128, D], F32, kind="ExternalInput").ap()
    ws1 = nc.dram_tensor("ws1", [16, 128, D], F32, kind="ExternalInput").ap()
    wsg = nc.dram_tensor("wsg", [16, 128, D], F32, kind="ExternalInput").ap()
    ws2 = nc.dram_tensor("ws2", [16, 128, D], F32, kind="ExternalInput").ap()
    gw = nc.dram_tensor("gw", [128, 112], F32, kind="ExternalInput").ap()
    bp = nc.dram_tensor("bp", [128, 80], F32, kind="ExternalInput").ap()
    out = nc.dram_tensor("out", [NSEQ, 16, 128, S], F32, kind="ExternalOutput").ap()
    qh = nc.dram_tensor("qh", [NSEQ, 16, 128, S], BF16, kind=dt_i).ap()
    kh = nc.dram_tensor("kh", [NSEQ, 16, 128, S], BF16, kind=dt_i).ap()
    vh = nc.dram_tensor("vh", [NSEQ, 16, 128, S], BF16, kind=dt_i).ap()
    acch = nc.dram_tensor("acch", [NSEQ, 16, 128, S], F32,
                          kind=("ExternalOutput" if (dbg and PH >= 5) else "Internal")).ap()
    dbgf = dbgy = dbgw = None
    if dbg and PH >= 2:
        dbgf = nc.dram_tensor("dbgf", [NSEQ, 16, 128, S], BF16, kind="ExternalOutput").ap()
    if dbg and PH >= 3:
        dbgy = nc.dram_tensor("dbgy", [NSEQ, 16, 128, S], BF16, kind="ExternalOutput").ap()
    if dbg and PH >= 4:
        dbgw = nc.dram_tensor("dbgw", [NSEQ, 7, S], BF16, kind="ExternalOutput").ap()

    ones_np = np.ones((128, 128), dtype=BF)
    mab_np = np.zeros((128, 128), dtype=BF)
    mab_np[0:64, 0:64] = 1
    mab_np[64:128, 64:128] = 1
    id_np = np.eye(128, dtype=np.float32)
    ind_np = np.zeros((7, 7 * 128), dtype=BF)
    for e in range(7):
        ind_np[e, e * 128:(e + 1) * 128] = 1
    c_ones = nc.inline_tensor(ones_np, "c_ones").ap()
    c_mab = nc.inline_tensor(mab_np, "c_mab").ap()
    c_id = nc.inline_tensor(id_np, "c_id").ap()
    c_ind = nc.inline_tensor(ind_np, "c_ind").ap()

    em = Em()
    sems = []

    def mkch(name):
        h = nc.semaphore(name)
        sem = h.__enter__()
        sems.append(h)
        return Ch(sem, name)

    ctx = contextlib.ExitStack()

    def sb(name, shape, dt):
        return ctx.enter_context(nc.sbuf_tensor(name, shape, dt))

    def pst(name, shape):
        return ctx.enter_context(nc.psum_tensor(name, shape, F32))

    S1 = sb("S1", [128, NCH * S], BF16)
    S2 = sb("S2", [128, NCH * S], BF16)
    expS = sb("expS", [128, 3072], BF16)
    qhc = [sb(f"qhc{i}", [128, S], BF16) for i in range(2)]
    khc = [sb(f"khc{i}", [128, S], BF16) for i in range(2)]
    vsb = [sb(f"vsb{i}", [128, S], BF16) for i in range(2)]
    wpA = [sb(f"wpA{i}", [128, D], BF16) for i in range(2)]
    wpB = [sb(f"wpB{i}", [128, D], BF16) for i in range(2)]
    gwt = sb("gwt", [128, 112], BF16)
    bpt = sb("bpt", [128, 80], F32)
    onest = sb("onest", [128, 128], BF16)
    mabt = sb("mabt", [128, 128], BF16)
    idt = sb("idt", [128, 128], F32)
    ind7t = sb("ind7t", [7, 896], BF16)
    esb = [sb(f"esb{i}", [128, TS], BF16) for i in range(2)]
    stmp = [sb(f"stmp{i}", [128, TS], F32) for i in range(2)]
    recipb = [sb(f"recip{i}", [128, TS], F32) for i in range(2)]
    meanb = sb("meanb", [128, TS], F32)
    rstdb = sb("rstdb", [128, TS], F32)
    stage = [sb(f"stage{i}", [128, TS], F32) for i in range(2)]
    astg = recipb
    ostg = [stage[0][:], stage[1][:]]
    wTt = sb("wTt", [7, S], BF16)
    wbt = sb("wbt", [128, S], BF16)
    gbb = sb("gbb", [128, 7], F32)
    bbb = sb("bbb", [128, 7], F32)
    grow = sb("grow", [1, 7], BF16)
    brow = sb("brow", [1, 7], BF16)
    g1 = sb("g1", [128, 7], F32)
    g2 = sb("g2", [128, 7], F32)
    g3 = sb("g3", [128, 7], F32)
    g4 = sb("g4", [128, 7], F32)
    gs1 = sb("gs1", [128, 1], F32)
    gs2 = sb("gs2", [128, 1], F32)
    wtok = sb("wtok", [128, 7], F32)
    tr1 = sb("tr1", [128, 1], F32)
    qn2a = sb("qn2a", [128, 16], F32)
    kn2a = sb("kn2a", [128, 16], F32)
    pm16 = sb("pm16", [128, 16], F32)
    c16 = sb("c16", [16, 1], F32)
    crow = sb("crow", [1, 16], BF16)
    cba = sb("cba", [128, 16], F32)

    PA = [pst("PA0", [128, 1024]), pst("PA1", [128, 1024])]
    PB = [pst(f"PB{i}", [128, TS]) for i in range(4)]

    cXL = mkch("cXL")
    cW = {"A0": mkch("cWA0"), "A1": mkch("cWA1"), "B0": mkch("cWB0"), "B1": mkch("cWB1")}
    cSP = [mkch("cSP0"), mkch("cSP1")]
    cHL = [mkch("cHL0"), mkch("cHL1")]
    cAS = [mkch("cAS0"), mkch("cAS1")]
    cAL = [mkch("cAL0"), mkch("cAL1")]
    cOS = [mkch("cOS0"), mkch("cOS1")]
    cMS = mkch("cMS")
    cDB = mkch("cDB")
    eAC = mkch("eAC")
    eDV = mkch("eDV")
    em.autochain = {"act": eAC, "dve": eDV}
    eEV = eEX = eSQ = eSI = eAC
    eCd = eRD = eNM = eFE = eGW = eWS = eWBS = eAP = eSG = eR2 = eY2 = eDV
    ePQ = mkch("ePQ")
    eSC = mkch("eSC")
    eAV = mkch("eAV")
    eHC = mkch("eHC")
    eC = mkch("eC")
    eAT = mkch("eAT")
    eSM = mkch("eSM")
    eGL = mkch("eGL")
    eWT = mkch("eWT")
    eWB = mkch("eWB")
    eH1 = mkch("eH1")
    eHG = mkch("eHG")
    eW2 = mkch("eW2")
    eP0 = mkch("eP0")
    ePS = mkch("ePS")

    W = em.wait
    O = em.op

    for dst, src in ((onest, c_ones), (mabt, c_mab), (idt, c_id), (ind7t, c_ind),
                     (gwt, gw), (bpt, bp), (grow, bp[0:1, 64:71]), (brow, bp[0:1, 71:78])):
        O("pool", "dma_start", out=dst[:], in_=src, incs=[(cMS, 16)])
    W("pe", cMS)
    O("pe", "matmul", PB[0][:, 0:7], onest[0:1, 0:128], grow[:], start=True, stop=True,
      skip_group_check=True, incs=[(eP0, 1)])
    O("pe", "matmul", PB[1][:, 0:7], onest[0:1, 0:128], brow[:], start=True, stop=True,
      skip_group_check=True, incs=[(eP0, 1)])
    W("dve", eP0)
    O("dve", "tensor_copy", gbb[:], PB[0][:, 0:7], incs=[(eCd, 1)])
    O("dve", "tensor_copy", bbb[:], PB[1][:, 0:7], incs=[(eCd, 1)])
    em.release(("PB", 0), eCd)
    em.release(("PB", 1), eCd)

    def ln_stats_finalize():
        W("act", eDV)
        W("dve", eSM)
        O("dve", "tensor_scalar_mul", meanb[:], PB[2][:, :], 1.0 / 2048.0)
        O("dve", "tensor_tensor", rstdb[:], meanb[:], meanb[:], ALU.mult)
        O("dve", "scalar_tensor_tensor", rstdb[:], PB[3][:, :], 1.0 / 2048.0, rstdb[:],
          ALU.mult, ALU.subtract)
        em.release(("PB", 2), eDV)
        em.release(("PB", 3), eDV)
        W("act", eDV)
        O("act", "activation", rstdb[:], rstdb[:], AF.Ln, bias=bpt[:, 78:79], scale=1.0)
        O("act", "activation", rstdb[:], rstdb[:], AF.Exp, bias=0.0, scale=-0.5)
        return eAC.n

    def ln_stats_pass(getr, rdy=None):
        em.acquire("pe", ("PB", 2))
        em.acquire("pe", ("PB", 3))
        for m in range(16):
            epar = m % 2
            if rdy is not None:
                W("act", rdy[m][0], rdy[m][1])
            em.acquire("act", ("esb", epar))
            O("act", "activation", esb[epar][:], getr(m), AF.Square, incs=[(eSQ, 1)])
            W("pe", eSQ)
            O("pe", "matmul", PB[2][:, :], onest[:], getr(m), start=(m == 0),
              stop=(m == 15), skip_group_check=True)
            O("pe", "matmul", PB[3][:, :], onest[:], esb[epar][:], start=(m == 0),
              stop=(m == 15), skip_group_check=True, incs=[(eSM, 1)])
            em.release(("esb", epar), eSM)
        return ln_stats_finalize()

    def emit_p6_st(ps, pstri, ps1t, prdy, st):
        W("sp", cAS[0], prdy[st][0])
        W("sp", cAS[1], prdy[st][1])
        rdy = []
        for m in range(16):
            apar = m % 2
            em.acquire("sp", ("rbuf", apar))
            O("sp", "dma_start", out=astg[apar][:],
              in_=acch[ps, m, :, st * TS:(st + 1) * TS], incs=[(cAL[apar], 16)])
            W("dve", cAL[apar])
            O("dve", "tensor_tensor", ps1t(m, st), astg[apar][:], ps1t(m, st),
              ALU.add)
            rdy.append((eDV, eDV.n))
            em.release(("rbuf", apar), eDV)
        ssv = ln_stats_pass(lambda m: ps1t(m, st), rdy)
        for m in range(16):
            opar = m % 2
            W("dve", eAC, ssv)
            em.acquire("dve", ("stage", opar))
            O("dve", "tensor_tensor", stmp[0][:], ps1t(m, st), meanb[:], ALU.subtract)
            O("dve", "tensor_tensor", ostg[opar], stmp[0][:], rstdb[:], ALU.mult)
            W("sp", eY2)
            O("sp", "dma_start", out=out[ps, m, :, st * TS:(st + 1) * TS],
              in_=ostg[opar], incs=[(cOS[opar], 16)])
            em.release(("stage", opar), cOS[opar])
        if st == NST - 1:
            for c in range(NCH):
                em.release(("buf", pstri, c), eDV)

    for s in range(nseq if PH >= 1 else 0):
        stri = s % 2
        ftsi = 1 - stri
        STR = S1 if stri == 0 else S2
        FTS = S2 if stri == 0 else S1

        def s1t(c, st, _B=STR):
            a = c * S + st * TS
            return _B[:, a:a + TS]

        def s2t(c, st, _B=FTS):
            a = c * S + st * TS
            return _B[:, a:a + TS]

        for c in range(NCH):
            em.acquire("pool", ("buf", stri, c))
            O("pool", "dma_start", out=STR[:, c * S:(c + 1) * S],
              in_=xT[s, c * 128:(c + 1) * 128, :], incs=[(cXL, 16)])
        xl_done = cXL.n

        p1b = 0
        ep1 = 0
        np1 = 0
        nsq = [0]
        npend = []

        def flush_norm(k=1):
            nonlocal np1
            for _ in range(k):
                if not npend:
                    return
                sq, dst, first, sq_dv = npend.pop(0)
                pa_ = np1 % 2
                np1 += 1
                em.acquire("pe", ("PA", pa_))
                W("pe", eDV, sq_dv)
                O("pe", "matmul", PA[pa_][:, 0:512], mabt[:], sq, start=True,
                  stop=True, skip_group_check=True, incs=[(eC, 1)])
                em.release(("wbt", slot), eC)
                W("dve", eC)
                O("dve", "reduce_max", tr1[:], PA[pa_][:, 0:512], axis=AX.X)
                em.release(("PA", pa_), eDV)
                if first:
                    O("dve", "tensor_copy", dst, tr1[:])
                else:
                    O("dve", "tensor_max", dst, tr1[:], dst)

        trip = ((wq, qh, 0), (wk, kh, 16), (wv, vh, 32))
        spill_rdy = {}
        cb_rdy = {}
        qk_rdy = {}
        p1c = [0]

        def emit_cb(m):
            O("dve", "tensor_tensor", tr1[:], qn2a[:, m:m + 1], kn2a[:, m:m + 1],
              ALU.add)
            O("dve", "tensor_scalar_mul", tr1[:], tr1[:], -0.5 * ISCALE)
            em.acquire("pe", ("PA", 0))
            W("pe", eDV)
            O("pe", "transpose", PA[0][0:1, 0:128], tr1[:], idt[:], incs=[(eC, 1)])
            W("dve", eC)
            O("dve", "tensor_reduce", c16[0:1, 0:1], PA[0][0:1, 0:128], AX.X, ALU.min)
            em.release(("PA", 0), eDV)
            O("dve", "tensor_copy", crow[0:1, 0:1], c16[0:1, 0:1])
            em.acquire("pe", ("PA", 1))
            W("pe", eDV)
            O("pe", "matmul", PA[1][:, 0:1], onest[0:1, 0:128], crow[0:1, 0:1],
              start=True, stop=True, skip_group_check=True, incs=[(eC, 1)])
            W("dve", eC)
            O("dve", "tensor_copy", cba[:, m:m + 1], PA[1][:, 0:1])
            em.release(("PA", 1), eDV)

        def emit_p2_loads(hc):
            par = hc % 2
            W("sp", cSP[0], spill_rdy[hc][0])
            W("sp", cSP[1], spill_rdy[hc][1])
            em.acquire("sp", ("hcv", par))
            for j in range(16):
                O("sp", "dma_start_transpose", vsb[par][:, j * 128:(j + 1) * 128],
                  vh[s, hc, :, j * 128:(j + 1) * 128], incs=[(cHL[par], 16)])
            return cHL[par].n

        def p2_gen(hc, hl_done):
            par = hc % 2
            W("pe", cHL[par], hl_done)
            W("pe", eDV, qk_rdy[hc])
            W("act", eDV, cb_rdy[hc])
            pend = [None]

            def emit_rhoav(it):
                st_, j_, slot_, exn = it
                pr, pf = 2, 3
                W("pe", eEX, exn)
                if j_ == 0:
                    em.acquire("pe", ("PB", pr))
                    em.acquire("pe", ("PB", pf))
                eslot = expS[:, slot_ * 1024:(slot_ + 1) * 1024]
                O("pe", "matmul", PB[pr][0:64, :], onest[:, 0:64], eslot[:, 0:512],
                  start=(j_ == 0), stop=(j_ == 15), skip_group_check=True,
                  tile_position=(0, 0))
                O("pe", "matmul", PB[pr][64:128, :], onest[:, 64:128], eslot[:, 512:1024],
                  start=(j_ == 0), stop=(j_ == 15), skip_group_check=True,
                  tile_position=(0, 64))
                O("pe", "matmul", PB[pf][0:64, :], vsb[par][:, j_ * 128:j_ * 128 + 64],
                  eslot[:, 0:512], start=(j_ == 0), stop=(j_ == 15),
                  skip_group_check=True, tile_position=(0, 0), incs=[(eAV, 1)])
                O("pe", "matmul", PB[pf][64:128, :], vsb[par][:, j_ * 128 + 64:(j_ + 1) * 128],
                  eslot[:, 512:1024], start=(j_ == 0), stop=(j_ == 15),
                  skip_group_check=True, tile_position=(0, 64), incs=[(eAV, 1)])
                em.release(("expS", slot_), eAV)
                if j_ == 15:
                    st2par = st_ % 2
                    W("dve", eAV)
                    O("dve", "tensor_copy", s2t(hc, st_), PB[pf][:, :])
                    em.release(("PB", pf), eDV)
                    em.acquire("dve", ("rbuf", st2par))
                    O("dve", "reciprocal", recipb[st2par][:], PB[pr][:, :])
                    em.release(("PB", pr), eDV)
                    O("dve", "tensor_tensor", s2t(hc, st_), s2t(hc, st_),
                      recipb[st2par][:], ALU.mult, incs=[(eFE, 1)])
                    em.release(("rbuf", st2par), eFE)

            glb = 0
            for st in range(NST):
                for j in range(16):
                    slot = glb % 3
                    pa = glb % 2
                    glb += 1
                    em.acquire("pe", ("PA", pa))
                    O("pe", "matmul", PA[pa][:, 0:512], khc[par][0:64, j * 128:(j + 1) * 128],
                      qhc[par][0:64, st * TS:(st + 1) * TS], start=True, stop=True,
                      skip_group_check=True, tile_position=(0, 0))
                    O("pe", "matmul", PA[pa][:, 512:1024],
                      khc[par][64:128, j * 128:(j + 1) * 128],
                      qhc[par][64:128, st * TS:(st + 1) * TS], start=True, stop=True,
                      skip_group_check=True, tile_position=(64, 0), incs=[(eSC, 1)])
                    W("act", eSC)
                    em.acquire("act", ("expS", slot))
                    O("act", "activation", expS[:, slot * 1024:(slot + 1) * 1024],
                      PA[pa][:, :], AF.Exp, bias=cba[:, hc:hc + 1], scale=ISCALE,
                      incs=[(eEX, 1)])
                    em.release(("PA", pa), eEX)
                    if pend[0] is not None:
                        emit_rhoav(pend[0])
                    pend[0] = (st, j, slot, eEX.n)
                if st == NST - 1:
                    emit_rhoav(pend[0])
                    em.release(("hcq", par), eSC)
                    em.release(("hcv", par), eAV)
                yield

        for m in range(nmch):
            gen = None
            if PH >= 2 and m >= 1 and nmats == 3:
                gen = p2_gen(m - 1, emit_p2_loads(m - 1))
            gi = 0
            hcpar = m % 2
            for mat, spill, bcol in trip[:nmats]:
                par = str(p1c[0] % 2)
                wb_ = wpA[p1c[0] % 2]
                p1c[0] += 1
                em.acquire("pool", ("wpA", par))
                O("pool", "dma_start", out=wb_[:], in_=mat[m],
                  incs=[(cW["A" + par], 16)])
                W("pe", cW["A" + par])
                W("pe", cXL, xl_done)
                if mat is wq:
                    em.acquire("dve", ("hcq", hcpar))
                for st in range(NST):
                    bk = p1b % 2
                    p1b += 1
                    em.acquire("pe", ("PB", bk))
                    for kc in range(16):
                        O("pe", "matmul", PB[bk][:, :], wb_[:, kc * 128:(kc + 1) * 128],
                          s1t(kc, st), start=(kc == 0), stop=(kc == 15),
                          skip_group_check=True, incs=[(ePQ, 1)] if kc == 15 else [])
                    W("dve", ePQ)
                    if mat is wv:
                        epar = ep1 % 2
                        ep1 += 1
                        em.acquire("dve", ("esb", epar))
                        O("dve", "tensor_scalar_add", esb[epar][:], PB[bk][:, :],
                          bpt[:, bcol + m:bcol + m + 1])
                        evac_dv = eDV.n
                        em.release(("PB", bk), eDV)
                        W("sp", eDV, evac_dv)
                        O("sp", "dma_start", out=spill[s, m, :, st * TS:(st + 1) * TS],
                          in_=esb[epar][:], incs=[(cSP[epar], 16)])
                        em.release(("esb", epar), cSP[epar])
                    else:
                        dstb = qhc[hcpar] if mat is wq else khc[hcpar]
                        O("dve", "tensor_scalar_add", dstb[:, st * TS:(st + 1) * TS],
                          PB[bk][:, :], bpt[:, bcol + m:bcol + m + 1])
                        evac_dv = eDV.n
                        em.release(("PB", bk), eDV)
                        if dbg:
                            W("sp", eDV, evac_dv)
                            O("sp", "dma_start",
                              out=spill[s, m, :, st * TS:(st + 1) * TS],
                              in_=dstb[:, st * TS:(st + 1) * TS], incs=[(cSP[0], 16)])
                        slot = nsq[0] % 4
                        nsq[0] += 1
                        em.acquire("dve", ("wbt", slot))
                        sq = wbt[:, slot * 512:(slot + 1) * 512]
                        O("dve", "tensor_tensor", sq,
                          dstb[:, st * TS:(st + 1) * TS],
                          dstb[:, st * TS:(st + 1) * TS], ALU.mult)
                        npend.append((sq, (qn2a if mat is wq else kn2a)[:, m:m + 1],
                                      st == 0, eDV.n))
                    flush_norm()
                    gi += 1
                    if gen is not None and gi % 3 == 0:
                        next(gen)
                em.release(("wpA", par), ePQ)
                if mat is wk:
                    qk_rdy[m] = eDV.n
            if nmats == 3 and PH >= 2:
                flush_norm(len(npend))
                emit_cb(m)
                cb_rdy[m] = eDV.n
            spill_rdy[m] = (cSP[0].n, cSP[1].n)
        flush_norm(len(npend))

        sp_q0 = cSP[0].n
        sp_q1 = cSP[1].n
        if PH < 2:
            for c in range(NCH):
                em.release(("buf", stri, c), ePQ)
            continue
        gen = p2_gen(nmch - 1, emit_p2_loads(nmch - 1))
        for _ in range(NST):
            next(gen)

        fe_all = eFE.n
        if dbg and PH >= 2:
            W("sp", eFE, fe_all)
            for c in range(NCH):
                O("sp", "dma_start", out=dbgf[s, c], in_=FTS[:, c * S:(c + 1) * S],
                  incs=[(cDB, 16)])
            W("dve", cDB)
        if PH < 3:
            for c in range(NCH):
                em.release(("buf", stri, c), cDB if dbg else eFE)
                em.release(("buf", ftsi, c), cDB if dbg else eFE)
            continue

        p3b = 0
        for m in range(16):
            par = str(m % 2)
            em.acquire("pool", ("wpA", par))
            O("pool", "dma_start", out=wpA[m % 2][:], in_=wo[m],
              incs=[(cW["A" + par], 16)])
            W("pe", cW["A" + par])
            W("pe", eFE, fe_all)
            for st in range(NST):
                bk = p3b % 2
                p3b += 1
                em.acquire("pe", ("PB", bk))
                for kc in range(16):
                    O("pe", "matmul", PB[bk][:, :], wpA[m % 2][:, kc * 128:(kc + 1) * 128],
                      s2t(kc, st), start=(kc == 0), stop=(kc == 15),
                      skip_group_check=True, incs=[(eAT, 1)] if kc == 15 else [])
                W("dve", eAT)
                O("dve", "scalar_tensor_tensor", s1t(m, st), PB[bk][:, :],
                  bpt[:, 48 + m:48 + m + 1], s1t(m, st), ALU.add, ALU.add,
                  incs=[(eRD, 1)])
                em.release(("PB", bk), eRD)
            em.release(("wpA", par), eAT)
        for st in range(NST):
            W("act", eDV)
            ssv = ln_stats_pass(lambda m: s1t(m, st))
            for m in range(16):
                W("dve", eAC, ssv)
                O("dve", "tensor_tensor", stmp[0][:], s1t(m, st), meanb[:], ALU.subtract)
                O("dve", "tensor_tensor", s1t(m, st), stmp[0][:], rstdb[:], ALU.mult)
        nm_all = eNM.n
        if dbg and PH >= 3:
            W("sp", eNM, nm_all)
            for c in range(NCH):
                O("sp", "dma_start", out=dbgy[s, c], in_=STR[:, c * S:(c + 1) * S],
                  incs=[(cDB, 16)])
            W("dve", cDB)
        if PH < 4:
            for c in range(NCH):
                em.release(("buf", stri, c), cDB if dbg else eNM)
                em.release(("buf", ftsi, c), cDB if dbg else eNM)
            continue

        for sc in range(16):
            bk = sc % 2
            em.acquire("pe", ("PB", bk))
            W("pe", eNM, nm_all)
            for kc in range(16):
                O("pe", "matmul", PB[bk][:, 0:7],
                  STR[:, kc * S + sc * 128:kc * S + (sc + 1) * 128],
                  gwt[:, kc * 7:(kc + 1) * 7], start=(kc == 0), stop=(kc == 15),
                  skip_group_check=True, incs=[(eGL, 1)] if kc == 15 else [])
            W("dve", eGL)
            O("dve", "tensor_tensor", g1[:], PB[bk][:, 0:7], gbb[:], ALU.add,
              incs=[(eCd, 1)])
            em.release(("PB", bk), eCd)
            O("dve", "reduce_max", gs1[:], g1[:], axis=AX.X)
            O("dve", "tensor_single_scalar", g1[:], g1[:], gs1[:], ALU.subtract,
              incs=[(eCd, 1)])
            W("act", eCd)
            O("act", "activation", g2[:], g1[:], AF.Exp, bias=0.0, scale=1.0,
              incs=[(eSQ, 1)])
            W("dve", eSQ)
            O("dve", "reduce_sum", gs1[:], g2[:], axis=AX.X)
            O("dve", "reciprocal", gs2[:], gs1[:])
            O("dve", "tensor_single_scalar", g2[:], g2[:], gs2[:], ALU.mult)
            O("dve", "tensor_tensor", g4[:], g2[:], bbb[:], ALU.add)
            O("dve", "tensor_copy", g1[:], g4[:])
            for _ in range(2):
                O("dve", "reduce_max", gs1[:], g1[:], axis=AX.X)
                O("dve", "tensor_single_scalar", g3[:], g1[:], gs1[:], ALU.is_ge)
                O("dve", "scalar_tensor_tensor", g1[:], g3[:], -BIG, g1[:],
                  ALU.mult, ALU.add)
            O("dve", "reduce_max", gs1[:], g1[:], axis=AX.X)
            O("dve", "tensor_single_scalar", g3[:], g4[:], gs1[:], ALU.is_ge)
            O("dve", "tensor_tensor", g3[:], g2[:], g3[:], ALU.mult)
            O("dve", "reduce_sum", gs1[:], g3[:], axis=AX.X)
            O("dve", "reciprocal", gs2[:], gs1[:])
            O("dve", "tensor_single_scalar", wtok[:], g3[:], gs2[:], ALU.mult,
              incs=[(eGW, 1)])
            bk2 = 2 + sc % 2
            em.acquire("pe", ("PB", bk2))
            W("pe", eGW)
            O("pe", "transpose", PB[bk2][0:7, 0:128], wtok[:], idt[:], incs=[(eWT, 1)])
            W("dve", eWT)
            O("dve", "tensor_copy", wTt[:, sc * 128:(sc + 1) * 128], PB[bk2][0:7, 0:128],
              incs=[(eWS, 1)])
            em.release(("PB", bk2), eWS)
        ws_all = eWS.n
        if dbg and PH >= 4:
            W("sp", eWS, ws_all)
            O("sp", "dma_start", out=dbgw[s], in_=wTt[:], incs=[(cDB, 16)])
            W("dve", cDB)
        if PH < 5:
            for c in range(NCH):
                em.release(("buf", stri, c), cDB if dbg else eWS)
                em.release(("buf", ftsi, c), cDB if dbg else eWS)
            continue

        acc_rdy = {}
        for e in range(8):
            routed = e > 0
            if routed:
                for st in range(NST):
                    bk = 2 + st % 2
                    em.acquire("pe", ("PB", bk))
                    W("pe", eWS, ws_all)
                    em.acquire("pe", ("wbt", st))
                    O("pe", "matmul", PB[bk][:, :], ind7t[:, (e - 1) * 128:e * 128],
                      wTt[:, st * TS:(st + 1) * TS], start=True, stop=True,
                      skip_group_check=True, incs=[(eWB, 1)])
                    W("dve", eWB)
                    em.acquire("dve", ("wbt", st))
                    O("dve", "tensor_copy", wbt[:, st * TS:(st + 1) * TS], PB[bk][:, :],
                      incs=[(eWBS, 1)])
                    em.release(("PB", bk), eWBS)
            mats = (w1[e - 1], wg[e - 1], w2[e - 1]) if routed else (ws1, wsg, ws2)
            for m in range(16):
                parA = str(m % 2)
                if m == 0 and e > 0:
                    pass
                else:
                    em.acquire("pool", ("wpA", parA))
                    O("pool", "dma_start", out=wpA[m % 2][:], in_=mats[0][m],
                      incs=[(cW["A" + parA], 16)])
                    em.acquire("pool", ("wpB", parA))
                    O("pool", "dma_start", out=wpB[m % 2][:], in_=mats[1][m],
                      incs=[(cW["B" + parA], 16)])
                W("pe", cW["A" + parA])
                W("pe", cW["B" + parA])
                for st in range(NST):
                    b1 = st % 2
                    b2 = 2 + st % 2
                    em.acquire("pe", ("PB", b1))
                    for kc in range(16):
                        O("pe", "matmul", PB[b1][:, :], wpA[m % 2][:, kc * 128:(kc + 1) * 128],
                          s1t(kc, st), start=(kc == 0), stop=(kc == 15),
                          skip_group_check=True, incs=[(eH1, 1)] if kc == 15 else [])
                    W("act", eH1)
                    em.acquire("act", ("stmp", st % 2))
                    O("act", "activation", stmp[st % 2][:], PB[b1][:, :], AF.Silu,
                      bias=0.0, scale=1.0, incs=[(eSI, 1)])
                    em.release(("PB", b1), eSI)
                    em.acquire("pe", ("PB", b2))
                    for kc in range(16):
                        O("pe", "matmul", PB[b2][:, :], wpB[m % 2][:, kc * 128:(kc + 1) * 128],
                          s1t(kc, st), start=(kc == 0), stop=(kc == 15),
                          skip_group_check=True, incs=[(eHG, 1)] if kc == 15 else [])
                    W("dve", eSI)
                    W("dve", eHG)
                    if routed:
                        W("dve", eWBS)
                        em.acquire("dve", ("esb", st % 2))
                        O("dve", "tensor_tensor", esb[st % 2][:], stmp[st % 2][:],
                          PB[b2][:, :], ALU.mult)
                        O("dve", "tensor_tensor", s2t(m, st), esb[st % 2][:],
                          wbt[:, st * TS:(st + 1) * TS], ALU.mult, incs=[(eAP, 1)])
                        em.release(("esb", st % 2), eAP)
                    else:
                        O("dve", "tensor_tensor", s2t(m, st), stmp[st % 2][:],
                          PB[b2][:, :], ALU.mult, incs=[(eAP, 1)])
                    em.release(("PB", b2), eAP)
                    em.release(("stmp", st % 2), eAP)
                em.release(("wpA", parA), eH1)
                em.release(("wpB", parA), eHG)
            ap_all = eAP.n
            for st in range(NST):
                em.release(("wbt", st), (eAP, ap_all))
            nxt = (w1[e], wg[e]) if e < 7 else None
            last = e == 7
            outer = list(range(NST)) if last else [None]
            pcnt = [0]

            def _w2_panels(mp, st_):
                parA = str(pcnt[0] % 2)
                pcnt[0] += 1
                em.acquire("pool", ("wpA", parA))
                O("pool", "dma_start", out=wpA[int(parA)][:], in_=mats[2][2 * mp],
                  incs=[(cW["A" + parA], 16)])
                em.acquire("pool", ("wpB", parA))
                O("pool", "dma_start", out=wpB[int(parA)][:], in_=mats[2][2 * mp + 1],
                  incs=[(cW["B" + parA], 16)])
                return parA

            def _w2_unit(mp, st, parA):
                wpa, wpb = wpA[int(parA)], wpB[int(parA)]
                W("pe", cW["A" + parA])
                W("pe", cW["B" + parA])
                W("pe", eAP, ap_all)
                for half, wp in ((0, wpa), (1, wpb)):
                    spar = (mp * 2 + st * 2 + half) % 2
                    bk = half
                    em.acquire("pe", ("PB", bk))
                    for kc in range(16):
                        O("pe", "matmul", PB[bk][:, :], wp[:, kc * 128:(kc + 1) * 128],
                          s2t(kc, st), start=(kc == 0), stop=(kc == 15),
                          skip_group_check=True, incs=[(eW2, 1)] if kc == 15 else [])
                    W("dve", eW2)
                    em.acquire("dve", ("stage", spar))
                    O("dve", "tensor_copy", stage[spar][:], PB[bk][:, :],
                      incs=[(eSG, 1)])
                    em.release(("PB", bk), eSG)
                    W("pool", eSG)
                    dst = acch[s][2 * mp + half, :, st * TS:(st + 1) * TS]
                    kw = {} if e == 0 else {"accum_op": ALU.add}
                    O("pool", "dma_start", out=dst, in_=stage[spar][:],
                      incs=[(cAS[spar], 16)], **kw)
                    em.release(("stage", spar), cAS[spar])
                em.release(("wpA", parA), eW2)
                em.release(("wpB", parA), eW2)

            if not last:
                par_next = _w2_panels(0, None)
                for mp in range(8):
                    par_cur = par_next
                    if mp < 7:
                        par_next = _w2_panels(mp + 1, None)
                    else:
                        parA = str(pcnt[0] % 2)
                        em.acquire("pool", ("wpA", parA))
                        O("pool", "dma_start", out=wpA[int(parA)][:], in_=nxt[0][0],
                          incs=[(cW["A" + parA], 16)])
                        em.acquire("pool", ("wpB", parA))
                        O("pool", "dma_start", out=wpB[int(parA)][:], in_=nxt[1][0],
                          incs=[(cW["B" + parA], 16)])
                    for st in range(NST):
                        _w2_unit(mp, st, par_cur)
            else:
                par_next = _w2_panels(0, 0)
                for st in range(NST):
                    for mp in range(8):
                        par_cur = par_next
                        if not (st == NST - 1 and mp == 7):
                            par_next = _w2_panels((mp + 1) % 8, st if mp < 7 else st + 1)
                        _w2_unit(mp, st, par_cur)
                    acc_rdy[st] = (cAS[0].n, cAS[1].n)
                    emit_p6_st(s, stri, s1t, acc_rdy, st)
        for c in range(NCH):
            em.release(("buf", ftsi, c), eW2)



    W("sp", cOS[0])
    W("sp", cOS[1])
    if dbg:
        W("sp", cDB)
        W("sp", cAS[0])
        W("sp", cAS[1])
        W("sp", cSP[0])
        W("sp", cSP[1])

    with nc.Block() as block:
        @block.tensor
        def _(t):
            em.replay(t, em.q["pe"])

        @block.scalar
        def _(a):
            em.replay(a, em.q["act"])

        @block.vector
        def _(v):
            em.replay(v, em.q["dve"])

        @block.sync
        def _(sy):
            em.replay(sy, em.q["sp"])

        @block.gpsimd
        def _(g):
            em.replay(g, em.q["pool"])

    ctx.close()
    for h in reversed(sems):
        h.__exit__(None, None, None)
    ninstr = sum(len(v) for v in em.q.values())
    return nc, ninstr


def _pan(w):
    w = np.asarray(w, dtype=np.float32)
    return np.ascontiguousarray(
        w.reshape(16, 128, 16, 128).transpose(2, 1, 0, 3).reshape(16, 128, 2048))


def _prep_inputs(inputs):
    g = {k: np.asarray(v) for k, v in inputs.items()}
    shared = {
        "wq": _pan(g["L0_attn_Wq"]), "wk": _pan(g["L0_attn_Wk"]),
        "wv": _pan(g["L0_attn_Wv"]), "wo": _pan(g["L0_attn_Wo"]),
        "ws1": _pan(g["L0_ffn_Ws1"]), "wsg": _pan(g["L0_ffn_Wsg"]),
        "ws2": _pan(g["L0_ffn_Ws2"]),
        "w1": np.stack([_pan(g["L0_ffn_W1"][e]) for e in range(7)]),
        "wg": np.stack([_pan(g["L0_ffn_Wg"][e]) for e in range(7)]),
        "w2": np.stack([_pan(g["L0_ffn_W2"][e]) for e in range(7)]),
    }
    shared["gw"] = np.ascontiguousarray(
        np.asarray(g["L0_ffn_gate_w"], np.float32).reshape(16, 128, 7)
        .transpose(1, 0, 2).reshape(128, 112))
    bp = np.zeros((128, 80), np.float32)
    bp[:, 0:16] = np.asarray(g["L0_attn_bq"], np.float32).reshape(16, 128).T
    bp[:, 16:32] = np.asarray(g["L0_attn_bk"], np.float32).reshape(16, 128).T
    bp[:, 32:48] = np.asarray(g["L0_attn_bv"], np.float32).reshape(16, 128).T
    bp[:, 48:64] = np.asarray(g["L0_attn_bo"], np.float32).reshape(16, 128).T
    bp[0, 64:71] = np.asarray(g["L0_ffn_gate_b"], np.float32)
    bp[0, 71:78] = np.asarray(g["L0_ffn_biases"], np.float32)
    bp[:, 78] = 1.0e-5
    shared["bp"] = bp
    x = np.asarray(g["L0_attn_x"], np.float32)
    in_maps = []
    for c in range(8):
        m = dict(shared)
        m["xT"] = np.ascontiguousarray(x[2 * c:2 * c + 2].transpose(0, 2, 1))
        in_maps.append(m)
    return in_maps


def kernel(**inputs):
    in_maps = _prep_inputs(inputs)
    nc, _ = build_nc()
    res = run_bass_kernel_spmd(nc, in_maps, core_ids=list(range(8)))
    out = np.empty((16, 2048, 2048), np.float32)
    for c in range(8):
        o = np.asarray(res.results[c]["out"])
        out[2 * c:2 * c + 2] = o.reshape(2, 2048, 2048).transpose(0, 2, 1)
    return out
